# version 16
# baseline (speedup 1.0000x reference)
import contextlib

import numpy as np

import concourse.bass as bass
import concourse.mybir as mybir
from concourse.bass_utils import run_bass_kernel_spmd

F32 = mybir.dt.float32
BF16 = mybir.dt.bfloat16
AF = mybir.ActivationFunctionType
ALU = mybir.AluOpType
AX = mybir.AxisListType

S = 4096
D = 1024
NH = 16
DH = 64
TSB = 256
NSB = 16
DFF = 4096
RANK = 256
NC = 8

R_1024 = 1.0 / 1024.0
STOP_AFTER = None
GRPS = (0, 1, 2, 3)
NOHOOK = False
NOKV = False
NWG = 99


class Res:
    __slots__ = ("w", "rs", "name")

    def __init__(self, name=""):
        self.w = None
        self.rs = []
        self.name = name


class Task:
    __slots__ = ("tid", "eng", "emit", "deps", "is_dma", "chan", "sem_val")

    def __init__(self, tid, eng, emit, deps, is_dma, chan=None):
        self.tid = tid
        self.eng = eng
        self.emit = emit
        self.deps = deps
        self.is_dma = is_dma
        self.chan = chan
        self.sem_val = None


class Sched:
    ENGINES = ("pe", "act", "dve", "pool", "sync")

    def __init__(self):
        self.tasks = []

    def add(self, eng, emit, reads=(), writes=(), is_dma=False, extra=(),
            chan=None):
        deps = set(extra)
        for r in reads:
            if r.w is not None:
                deps.add(r.w)
        for r in writes:
            if r.w is not None:
                deps.add(r.w)
            deps.update(r.rs)
        t = Task(len(self.tasks), eng, emit, deps, is_dma, chan)
        self.tasks.append(t)
        for r in reads:
            r.rs.append(t.tid)
        for r in writes:
            r.w = t.tid
            r.rs = []
        return t.tid


def build_nc(n_g=NSB + 1):
    nc = bass.Bass("TRN2", target_bir_lowering=False)

    x_d = nc.dram_tensor("x", [D, S], F32, kind="ExternalInput").ap()
    out_d = nc.dram_tensor("out", [D, S], F32, kind="ExternalOutput").ap()
    wq_d = nc.dram_tensor("wq", [D, D], F32, kind="ExternalInput").ap()
    wk_d = nc.dram_tensor("wk", [D, D], F32, kind="ExternalInput").ap()
    wv_d = nc.dram_tensor("wv", [D, D], F32, kind="ExternalInput").ap()
    wo_d = nc.dram_tensor("wo", [D, D], F32, kind="ExternalInput").ap()
    u1_d = nc.dram_tensor("u1", [D, RANK], F32, kind="ExternalInput").ap()
    v1_d = nc.dram_tensor("v1", [RANK, DFF], F32, kind="ExternalInput").ap()
    b1_d = nc.dram_tensor("b1", [DFF], F32, kind="ExternalInput").ap()
    u2_d = nc.dram_tensor("u2", [DFF, RANK], F32, kind="ExternalInput").ap()
    v2_d = nc.dram_tensor("v2", [RANK, D], F32, kind="ExternalInput").ap()
    b2_d = nc.dram_tensor("b2", [D], F32, kind="ExternalInput").ap()

    A = nc.alloc_sbuf_tensor
    wq = A("wq_sb", [128, NC, D], BF16)
    wk = A("wk_sb", [128, NC, D], BF16)
    wv = A("wv_sb", [128, NC, D], BF16)
    wo = A("wo_sb", [128, NC, D], BF16)
    u1 = A("u1_sb", [128, NC, RANK], BF16)
    v1 = A("v1_sb", [128, 2, DFF], BF16)
    u2 = A("u2_sb", [128, 32, RANK], BF16)
    v2 = A("v2_sb", [128, 2, D], BF16)
    b1col = A("b1col", [128, 32], F32)
    b2col = A("b2col", [128, 8], F32)
    cqrow = A("cqrow", [1, D], BF16)
    ckrow = A("ckrow", [1, D], BF16)
    cvrow = A("cvrow", [1, D], BF16)
    cu1row = A("cu1row", [1, RANK], BF16)

    ones_col = A("ones_col", [128, 1], BF16)
    ones_f32 = A("ones_f32", [1, 128], F32)
    ident = A("ident", [128, 128], BF16)
    cst = A("cst", [128, 8], F32)
    negmh = A("negmh", [128, 16], F32)
    kcmaxr = A("kcmaxr", [1, 6, 8], F32)
    qcln = A("qcln", [128, 16], F32)
    mrow = A("mrow", [1, 64], F32)

    xr = A("x_ring", [128, 3, NC, TSB], F32)
    xbsq = A("xbsq", [128, NC, 512], BF16)
    xhat = A("xhat", [128, NC, TSB], BF16)
    negmr3 = A("negmr3_row", [1, TSB], BF16)
    scr = A("scr", [128, 2048], BF16)
    qr = A("q_ring", [128, 2, NC, TSB], BF16)
    kr = A("k_ring", [128, NC, 6, 128], BF16)
    vr = A("v_ring", [128, 6, D], BF16)
    pbuf = A("pbuf", [128, 2, 1024], BF16)
    x1hat = pbuf[:].rearrange("p a t -> p (a t)").rearrange(
        "p (c t) -> p c t", c=NC)
    dbuf = A("dbuf", [128, 2, 4, 128], BF16)
    zbuf = A("zbuf", [128, 2, NH], F32)
    zinv = A("zinv", [128, 2, NH], BF16)
    ctx = A("ctx_sb", [128, NC, TSB], BF16)
    p2sb = A("p2sb", [128, 2, TSB], BF16)
    q2sb = A("q2sb", [128, 2, TSB], BF16)
    rbsb = A("rb_sb", [128, TSB], F32)
    stats = A("stats_sb", [1, 512], F32)
    small = A("small_sb", [1, 768], F32)
    rstd = A("rstd_row", [1, TSB], F32)
    negmr = A("negmr_row", [1, TSB], BF16)
    negmrf = A("negmrf_row", [1, TSB], F32)

    P1 = nc.alloc_psum_tensor("P1", [128, 2048], F32)
    P2 = nc.alloc_psum_tensor("P2", [128, 2048], F32)

    CONSTS = {0: 0.125, 1: 0.5, 2: R_1024, 3: -R_1024, 4: 1e-5, 5: -0.5}
    for j, v in CONSTS.items():
        nc.const_aps.register(F32, float(v), cst[:, j:j + 1])

    sch = Sched()
    r_x = [Res(f"x{i}") for i in range(3)]
    r_xbq = Res("xbsq")
    r_xhat = Res("xhat")
    r_scr = [Res(f"scr{i}") for i in range(4)]
    r_q = [Res("q0"), Res("q1")]
    r_k = [Res(f"k{i}") for i in range(6)]
    r_v = [Res(f"v{i}") for i in range(6)]
    r_p = [Res(f"pbuf{i}") for i in range(4)]
    r_d = [Res("dbufA"), Res("dbufB")]
    r_z = Res("zbuf")
    r_zi = Res("zinv")
    r_nm = [Res(f"nm{i}") for i in range(16)]
    r_mh = Res("mhat")
    r_ctx = [Res("ctxA"), Res("ctxB")]
    r_p2sb = Res("p2sb")
    r_q2sb = Res("q2sb")
    r_rb = Res("rbsb")
    r_st = Res("stats")
    r_ln = Res("lnrows")
    r_mean = Res("meanrow")
    r_sb = [Res(f"P1b{i}") for i in range(4)]
    r_pA = r_sb[0]
    r_pB = r_sb[2]
    r_s4 = Res("P2bank4")
    r_s5 = Res("P2bank5")
    r_cx = [Res("P2ctx6"), Res("P2ctx7")]
    r_w = Res("weights")
    r_c = Res("consts")

    PING = [P1[:, 0:1024], P1[:, 1024:2048]]
    SBANK = [P1[:, i * 512:(i + 1) * 512] for i in range(4)]
    SCRS = [scr[:, i * 512:(i + 1) * 512] for i in range(4)]
    RPING = [r_pA, r_pB]
    B45 = [P2[:, 0:512], P2[:, 512:1024]]
    RB45 = [r_s4, r_s5]
    CTXPS = P2[:, 1024:2048]

    def t_init_memsets(eng):
        eng.memset(negmh[:], 0.0)
        eng.memset(kcmaxr[:], 1.0)
        eng.memset(qcln[:], 0.0)
        eng.memset(ones_col[:], 1.0)
        eng.memset(ones_f32[:], 1.0)
        last = None
        for j, v in CONSTS.items():
            last = eng.memset(cst[:, j:j + 1], float(v))
        return last
    sch.add("dve", t_init_memsets, writes=(r_c,))

    iota_scr = pbuf[:].rearrange("p a t -> p (a t)").bitcast(F32)

    def t_iota(eng):
        eng.iota(iota_scr[:, 0:128], pattern=[[0, 128]], channel_multiplier=1,
                 allow_small_or_imprecise_dtypes=True)
        return eng.iota(iota_scr[:, 128:256], pattern=[[1, 128]],
                        channel_multiplier=0,
                        allow_small_or_imprecise_dtypes=True)
    sch.add("pool", t_iota, writes=tuple(r_p))

    def t_ident(eng):
        return eng.tensor_tensor(ident[:], iota_scr[:, 0:128],
                                 iota_scr[:, 128:256], ALU.is_equal)
    sch.add("dve", t_ident, reads=tuple(r_p), writes=(r_c,))

    for wi_, (dst, src) in enumerate((
        (wq, wq_d.rearrange("(kc p) j -> p kc j", p=128)),
        (wk, wk_d.rearrange("(kc p) j -> p kc j", p=128)),
        (wv, wv_d.rearrange("(kc p) j -> p kc j", p=128)),
        (wo, wo_d.rearrange("(kc p) j -> p kc j", p=128)),
        (u1, u1_d.rearrange("(kc p) j -> p kc j", p=128)),
        (v1, v1_d.rearrange("(kc p) j -> p kc j", p=128)),
        (u2, u2_d.rearrange("(kc p) j -> p kc j", p=128)),
        (v2, v2_d.rearrange("(kc p) j -> p kc j", p=128)),
    )):
        sch.add("pool",
                (lambda d, s: lambda eng: eng.dma_start(d[:], s))(dst, src),
                writes=(r_w,), is_dma=True, chan=f"w{wi_}")

    b1stage = pbuf[:].rearrange("p a t -> p (a t)")[0:32, 0:128]
    b2stage = pbuf[:].rearrange("p a t -> p (a t)")[0:8, 128:256]
    sch.add("pool",
            lambda eng: eng.dma_start(b1stage,
                                      b1_d.rearrange("(c p) -> c p", p=128)),
            writes=tuple(r_p), is_dma=True, chan="w8")
    sch.add("pool",
            lambda eng: eng.dma_start(b2stage,
                                      b2_d.rearrange("(c p) -> c p", p=128)),
            writes=tuple(r_p), is_dma=True, chan="w9")

    def t_btr(eng):
        eng.matmul(P2[:, 0:32], b1stage, ident[0:32, 0:32],
                   start=True, stop=True)
        return eng.matmul(P2[:, 32:40], b2stage, ident[0:8, 0:8],
                          start=True, stop=True)
    sch.add("pe", t_btr, reads=tuple(r_p) + (r_c,), writes=(r_s4,))

    def t_bcol(eng):
        eng.tensor_copy(b1col[:], P2[:, 0:32])
        return eng.tensor_copy(b2col[:], P2[:, 32:40])
    sch.add("dve", t_bcol, reads=(r_s4,), writes=(r_w,))

    def t_crows_a(eng):
        last = None
        for wsb, base in ((wq, 0), (wk, 1024)):
            for half in range(2):
                for kc in range(NC):
                    last = eng.matmul(
                        P1[0:1, base + half * 512: base + (half + 1) * 512],
                        ones_col[:], wsb[:, kc, half * 512:(half + 1) * 512],
                        start=(kc == 0), stop=(kc == NC - 1))
        return last
    sch.add("pe", t_crows_a, reads=(r_w, r_c), writes=tuple(r_sb))

    def t_crows_a_evac(eng):
        eng.tensor_copy(cqrow[0:1, :], P1[0:1, 0:1024])
        return eng.tensor_copy(ckrow[0:1, :], P1[0:1, 1024:2048])
    sch.add("dve", t_crows_a_evac, reads=tuple(r_sb), writes=(r_w,))

    def t_crows_b(eng):
        last = None
        for half in range(2):
            for kc in range(NC):
                last = eng.matmul(P1[0:1, half * 512:(half + 1) * 512],
                                  ones_col[:],
                                  wv[:, kc, half * 512:(half + 1) * 512],
                                  start=(kc == 0), stop=(kc == NC - 1))
        for kc in range(NC):
            last = eng.matmul(P1[0:1, 1024:1280], ones_col[:], u1[:, kc, :],
                              start=(kc == 0), stop=(kc == NC - 1))
        return last
    sch.add("pe", t_crows_b, reads=(r_w, r_c) + tuple(r_sb), writes=tuple(r_sb))

    def t_crows_b_evac(eng):
        eng.tensor_copy(cvrow[0:1, :], P1[0:1, 0:1024])
        return eng.tensor_copy(cu1row[0:1, :], P1[0:1, 1024:1280])
    sch.add("dve", t_crows_b_evac, reads=tuple(r_sb), writes=(r_w,))

    rb_bc = P2[:, 512:768].rearrange("p (u t) -> p u t", u=1).broadcast_to(
        [128, NC, TSB])
    negmr_bc = P2[:, 768:1024].rearrange(
        "p (u t) -> p u t", u=1).broadcast_to([128, NC, TSB])
    xbv = xbsq[:].rearrange("p c (a t) -> p c a t", a=2)[:, :, 0, :]
    sqv_t = xbsq[:].rearrange("p c (a t) -> p c a t", a=2)[:, :, 1, :]

    def emit_ln_stats(with_negm, pre_negmr=None, negmr_dst=None,
                      mean_dst=None, evac_rb=False):
        def t_sq(eng):
            return eng.tensor_tensor(sqv_t, xbv, xbv, ALU.mult)
        sch.add("dve", t_sq, reads=(r_xbq,), writes=(r_xbq,))

        def t_sums(eng):
            last = None
            for c in range(NC):
                last = eng.matmul(P2[0:1, 0:512], ones_col[:], xbsq[:, c, :],
                                  start=(c == 0), stop=(c == NC - 1))
            return last
        sch.add("pe", t_sums, reads=(r_xbq, r_c), writes=(r_s4,))

        def t_stats_evac(eng):
            return eng.tensor_copy(stats[0:1, :], P2[0:1, 0:512])
        sch.add("dve", t_stats_evac, reads=(r_s4,), writes=(r_st,))

        sumv = stats[0:1, 0:TSB]
        sqv = stats[0:1, TSB:512]

        if mean_dst is not None:
            def t_mean(eng):
                return eng.tensor_single_scalar(mean_dst[0:1, :], sumv,
                                                -R_1024, ALU.mult)
            sch.add("dve", t_mean, reads=(r_st, r_c), writes=(r_mean,))

        def t_small_dve1(eng):
            return eng.scalar_tensor_tensor(small[0:1, 0:TSB], sumv, R_1024,
                                            sumv, ALU.mult, ALU.mult)
        sch.add("dve", t_small_dve1, reads=(r_st, r_c), writes=(r_st,))

        def t_small_dve2(eng):
            return eng.tensor_tensor(small[0:1, TSB:512], sqv,
                                     small[0:1, 0:TSB], ALU.subtract)
        sch.add("dve", t_small_dve2, reads=(r_st,), writes=(r_st,))

        def t_small_act1(eng):
            return eng.activation(small[0:1, 512:768], small[0:1, TSB:512],
                                  AF.Ln, scale=R_1024, bias=1e-5)
        sch.add("act", t_small_act1, reads=(r_st, r_c), writes=(r_st,))

        def t_small_act2(eng):
            return eng.activation(rstd[0:1, :], small[0:1, 512:768], AF.Exp,
                                  scale=-0.5)
        sch.add("act", t_small_act2, reads=(r_st, r_c), writes=(r_st, r_ln))

        if pre_negmr is not None:
            pre_negmr()

        if mean_dst is None:
            def t_negmr(eng):
                dst = negmrf[0:1, :] if with_negm else (
                    negmr_dst if negmr_dst is not None else negmr)[0:1, :]
                return eng.scalar_tensor_tensor(dst, sumv, -R_1024,
                                                rstd[0:1, :], ALU.mult,
                                                ALU.mult)
            sch.add("dve", t_negmr, reads=(r_st, r_ln, r_c), writes=(r_ln,))

        def t_bcast(eng):
            last = eng.matmul(P2[:, 512:768], ones_f32[:], rstd[0:1, :],
                              start=True, stop=True)
            if with_negm:
                last = eng.matmul(P2[:, 768:1024], ones_f32[:], negmrf[0:1, :],
                                  start=True, stop=True)
            return last
        sch.add("pe", t_bcast, reads=(r_ln, r_c), writes=(r_s5,))

        if with_negm or evac_rb:
            def t_rb_evac(eng):
                return eng.tensor_copy(rbsb[:], P2[:, 512:768])
            sch.add("dve", t_rb_evac, reads=(r_s5,), writes=(r_rb,))

    def proj_pair_unit(wsb, crow, p, evac_eng, evac_fn, evac_writes,
                       xsrc=None, negrow=None, rx=None, evac_reads=(),
                       rneg=None):
        bank = p % 2
        xsrc = xhat if xsrc is None else xsrc
        negrow = negmr if negrow is None else negrow
        rx = (r_xhat,) if rx is None else rx
        rneg = r_ln if rneg is None else rneg

        def t_mm(eng, p=p, bank=bank, wsb=wsb, crow=crow):
            last = None
            for o in range(2):
                oc = 2 * p + o
                dst = B45[bank][:, o * TSB:(o + 1) * TSB]
                for kc in range(NC):
                    last = eng.matmul(dst,
                                      wsb[:, kc, oc * 128:(oc + 1) * 128],
                                      xsrc[:, kc, :],
                                      start=(kc == 0), stop=False)
                last = eng.matmul(dst, crow[0:1, oc * 128:(oc + 1) * 128],
                                  negrow[0:1, :], start=False, stop=True)
            return last

        def emit():
            sch.add("pe", t_mm, reads=(r_w, rneg) + tuple(rx),
                    writes=(RB45[bank],))
            sch.add(evac_eng, evac_fn, reads=(RB45[bank],) + tuple(evac_reads),
                    writes=tuple(evac_writes))
        return emit

    r_kc = [Res(f"kc{i}") for i in range(6)]
    mh_scr = nc.alloc_sbuf_tensor("mh_scr", [128, 8], F32)

    def emit_ksq(g):
        s0 = (2 * g) % 6

        def t_ksq(eng, s0=s0):
            kv = kr[:, :, s0:s0 + 2, :]
            sv = scr[:].rearrange("p (c b q) -> p c b q", c=NC, b=2)
            return eng.tensor_tensor(sv, kv, kv, ALU.mult)
        sch.add("pool", t_ksq, reads=(r_k[s0], r_k[s0 + 1]),
                writes=tuple(r_scr))

        for b in range(2):
            def t_kcmm(eng, b=b):
                last = None
                sv = scr[:].rearrange("p (c b q) -> p c b q", c=NC, b=2)
                for c in range(NC):
                    last = eng.matmul(P2[0:1, c * 128:(c + 1) * 128],
                                      ones_col[:], sv[:, c, b, :],
                                      start=True, stop=True)
                return last
            sch.add("pe", t_kcmm, reads=tuple(r_scr) + (r_c,),
                    writes=(r_s4, r_s5))

            def t_kcred(eng, b=b, s0=s0):
                return eng.tensor_reduce(
                    kcmaxr[0:1, s0 + b, :].rearrange("a (c v) -> a c v", v=1),
                    P2[0:1, 0:1024].rearrange("a (c q) -> a c q", c=NC),
                    axis=AX.X, op=ALU.max)
            sch.add("dve", t_kcred, reads=(r_s4, r_s5),
                    writes=(r_kc[s0 + b],))

    def emit_negmh(sbx, qj, wi):
        qbx = 2 * sbx + qj
        wlist = [w for w in (qbx - 1, qbx) if 0 <= w <= 30]
        w = wlist[wi]
        idx = qj * 2 + wi

        def t_kw(eng, w=w, idx=idx):
            return eng.tensor_tensor(mrow[0:1, idx * 8:(idx + 1) * 8],
                                     kcmaxr[0:1, w % 6, :],
                                     kcmaxr[0:1, (w + 1) % 6, :], ALU.max)
        sch.add("dve", t_kw, reads=(r_kc[w % 6], r_kc[(w + 1) % 6]),
                writes=(r_mh,))

        def t_lnkw(eng, idx=idx):
            return eng.activation(mrow[0:1, 32 + idx * 8:40 + idx * 8],
                                  mrow[0:1, idx * 8:(idx + 1) * 8], AF.Ln)
        sch.add("act", t_lnkw, reads=(r_mh,), writes=(r_mh,))

        def t_bc(eng, idx=idx):
            return eng.matmul(P2[:, 1016:1024], ones_f32[:],
                              mrow[0:1, 32 + idx * 8:40 + idx * 8],
                              start=True, stop=True)
        sch.add("pe", t_bc, reads=(r_mh, r_c), writes=(r_s5,))

        def t_add(eng, qj=qj):
            return eng.tensor_tensor(mh_scr[:], qcln[:, qj * 8:qj * 8 + 8],
                                     P2[:, 1016:1024], ALU.add)
        sch.add("dve", t_add, reads=(r_mh, r_s5), writes=(r_mh,))

        def t_exp2(eng):
            return eng.activation(mh_scr[:], mh_scr[:], AF.Exp, scale=0.5)
        sch.add("act", t_exp2, reads=(r_mh, r_c), writes=(r_mh,))

        def t_red(eng, qj=qj, wi=wi):
            return eng.reduce_max(
                negmh[:, qj * 8 + wi * 4:qj * 8 + wi * 4 + 4].rearrange(
                    "p (g v) -> p g v", v=1),
                mh_scr[:].rearrange("p (g c) -> p g c", g=4),
                axis=AX.X, negate=True)
        nres = tuple(r_nm[qj * 8 + wi * 4 + grp] for grp in range(4))
        sch.add("dve", t_red, reads=(r_mh,), writes=nres + (r_mh,))

    def emit_qside_sq(g):
        qsl_ = g % 2

        def t_qsq(eng, qsl_=qsl_):
            pv = pbuf[:].rearrange("p a t -> p (a t)").rearrange(
                "p (c q) -> p c q", c=NC)
            return eng.tensor_tensor(pv, qr[:, qsl_], qr[:, qsl_], ALU.mult)
        sch.add("pool", t_qsq, reads=(r_q[qsl_],), writes=(r_p[0], r_p[1]))

    def emit_qside(g):
        def t_qn(eng):
            last = None
            pv = pbuf[:].rearrange("p a t -> p (a t)").rearrange(
                "p (c q) -> p c q", c=NC)
            for qblk in range(2):
                for c in range(NC):
                    last = eng.matmul(
                        P1[:, qblk * 8 + c:qblk * 8 + c + 1],
                        pv[:, c, qblk * 128:(qblk + 1) * 128],
                        ones_col[:], start=True, stop=True)
            return last
        sch.add("pe", t_qn, reads=(r_p[0], r_p[1], r_c), writes=(r_sb[0],))

        def t_qcln(eng):
            return eng.activation(qcln[:], P1[:, 0:16], AF.Ln)
        sch.add("act", t_qcln, reads=(r_sb[0],), writes=(r_mh,))

    n_sync_dma = [0]

    def dma_sync(src_ap, dst_ap, reads, writes, chan):
        def t(eng, s=src_ap, d=dst_ap):
            return eng.dma_start(d, s)
        tid = sch.add("sync", t, reads=reads, writes=writes, is_dma=True,
                      chan=chan)
        n_sync_dma[0] += 1
        return tid

    x_re = x_d.rearrange("(c p) t -> p c t", p=128)
    out_re = out_d.rearrange("(c p) t -> p c t", p=128)

    for g in range(n_g):
        kv_units = []
        q_units = []
        v_defer = []
        if g == 0:
            dma_sync(x_re[:, :, 0:TSB], xr[:, 0],
                     reads=(), writes=(r_x[0],), chan="ld0")
        if g + 1 < NSB:
            nsl = (g + 1) % 3
            dma_sync(x_re[:, :, (g + 1) * TSB:(g + 2) * TSB], xr[:, nsl],
                     reads=(), writes=(r_x[nsl],), chan=f"ld{nsl}")
        if g < NSB:
            sl = g % 3

            def t_cast(eng, sl=sl):
                return eng.tensor_copy(xbv, xr[:, sl])
            sch.add("dve", t_cast, reads=(r_x[sl],), writes=(r_xbq,))
            emit_ln_stats(with_negm=False)

            def t_xhat(eng, sl=sl):
                return eng.tensor_tensor(xhat[:], xr[:, sl], rb_bc, ALU.mult)
            sch.add("dve", t_xhat, reads=(r_x[sl], r_s5), writes=(r_xhat,))

            qsl = g % 2
            s0 = (2 * g) % 6

            for p in range(4):
                def ke_fn(eng, p=p, bank=p % 2, s0=s0):
                    return eng.tensor_copy(
                        kr[:, 2 * p:2 * p + 2, s0:s0 + 2, :],
                        B45[bank][:, :].rearrange(
                            "p (c b q) -> p c b q", c=2, b=2))
                kv_units.append(proj_pair_unit(wk, ckrow, p, "dve", ke_fn,
                                               (r_k[s0], r_k[s0 + 1])))
            for u in range(4):
                j, eh = u // 2, u % 2
                bank = u % 2 if u < 2 else 0

                def t_vmm(eng, j=j, eh=eh, bank=bank):
                    dst = B45[bank][:, :]
                    last = None
                    for kc in range(NC):
                        last = eng.matmul(
                            dst, xhat[:, kc, j * 128:(j + 1) * 128],
                            wv[:, kc, eh * 512:(eh + 1) * 512],
                            start=(kc == 0), stop=False)
                    last = eng.matmul(
                        dst, negmr[0:1, j * 128:(j + 1) * 128],
                        cvrow[0:1, eh * 512:(eh + 1) * 512],
                        start=False, stop=True)
                    return last

                def t_ve(eng, j=j, eh=eh, bank=bank, s0=s0):
                    return eng.activation(
                        vr[:, s0 + j, eh * 512:(eh + 1) * 512],
                        B45[bank][:, :], AF.Copy)

                def mk(vm=t_vmm, ve=t_ve, bank=bank, j=j, s0=s0):
                    def emit():
                        sch.add("pe", vm, reads=(r_w, r_xhat, r_ln),
                                writes=(RB45[bank],))
                        sch.add("act", ve, reads=(RB45[bank],),
                                writes=(r_v[s0 + j],))
                    return emit
                (kv_units if u < 2 else v_defer).append(mk())

            for p in range(4):
                def qe_fn(eng, p=p, bank=p % 2, qsl=qsl):
                    return eng.activation(
                        qr[:, qsl, 2 * p:2 * p + 2, :].rearrange(
                            "p c t -> p (c t)"),
                        B45[bank][:, :], AF.Copy, scale=0.125)
                q_units.append(proj_pair_unit(wq, cqrow, p, "act", qe_fn,
                                              (r_q[qsl],)))

        def pop_kv(n=1):
            for _ in range(n):
                if kv_units:
                    kv_units.pop(0)()

        def pop_q(n=1):
            for _ in range(n):
                if q_units:
                    q_units.pop(0)()

        if g >= 1:
            sb = g - 1
            sl = sb % 3
            qsl = sb % 2

            if STOP_AFTER == "preattn":
                continue
            wgs = []
            for qj in range(2):
                qb = 2 * sb + qj
                wins = [w for w in (qb - 1, qb) if 0 <= w <= 30]
                for wi, w in enumerate(wins):
                    wgs.append((qj, wi, w, len(wins)))

            def wg_front(si, qj, wi, w, nw, grp, qsl=qsl):
                pair = si % 2
                nmi = qj * 8 + wi * 4 + grp

                def t_smm(eng, qj=qj, grp=grp, w=w, pair=pair, qsl=qsl):
                    last = None
                    eng.matmul(PING[pair][:, 0:128], ident[:],
                               kr[:, grp * 2, w % 6, :], start=True, stop=True)
                    for parity in (0, 1):
                        if parity == 1:
                            eng.matmul(PING[pair][:, 256:384], ident[:],
                                       kr[:, grp * 2, w % 6, :],
                                       start=True, stop=True)
                        pb = 64 * parity
                        for hi in range(parity, 4, 2):
                            h = grp * 4 + hi
                            lhs = qr[pb:pb + 64, qsl, h // 2,
                                     qj * 128:(qj + 1) * 128]
                            if w % 6 <= 4:
                                dst = PING[pair][:, hi * 256:hi * 256 + 256]
                                last = eng.matmul(
                                    dst, lhs,
                                    kr[pb:pb + 64, h // 2,
                                       w % 6:w % 6 + 2, :].rearrange(
                                        "p b q -> p (b q)"),
                                    start=True, stop=True)
                                continue
                            for kbi, kb in enumerate((w, w + 1)):
                                dst = PING[pair][
                                    :, hi * 256 + kbi * 128:
                                    hi * 256 + kbi * 128 + 128]
                                last = eng.matmul(
                                    dst, lhs,
                                    kr[pb:pb + 64, h // 2, kb % 6, :],
                                    start=True, stop=True)
                    return last
                kres = (r_k[w % 6], r_k[(w + 1) % 6])
                sch.add("pe", t_smm, reads=kres + (r_q[qsl], r_c),
                        writes=(r_sb[2 * pair], r_sb[2 * pair + 1]))

                def t_exp(eng, pair=pair, nmi=nmi):
                    return eng.activation(pbuf[:, pair, :], PING[pair][:, :],
                                          AF.Exp,
                                          bias=negmh[:, nmi:nmi + 1])
                sch.add("act", t_exp,
                        reads=(r_sb[2 * pair], r_sb[2 * pair + 1],
                               r_nm[nmi], r_mh),
                        writes=(r_p[pair],))

                def t_z(eng, pair=pair, wi=wi, grp=grp):
                    return eng.tensor_reduce(
                        zbuf[:, wi, grp * 4:grp * 4 + 4].rearrange(
                            "p (h v) -> p h v", v=1),
                        pbuf[:, pair, :].rearrange("p (h t) -> p h t", h=4),
                        axis=AX.X, op=ALU.add)
                sch.add("dve", t_z, reads=(r_p[pair],), writes=(r_z,))

                def t_zinv(eng, wi=wi, grp=grp):
                    with nc.allow_low_precision(
                            reason="1/Z at bf16 matches bf16 P operand"):
                        return eng.reciprocal(
                            zinv[:, wi, grp * 4:grp * 4 + 4],
                            zbuf[:, wi, grp * 4:grp * 4 + 4])
                sch.add("dve", t_zinv, reads=(r_z,), writes=(r_zi,))

                def t_dbuild(eng, wi=wi, grp=grp, pair=pair):
                    iv = ident[:].rearrange(
                        "p (v q) -> p v q", v=1).broadcast_to([128, 4, 128])
                    zz = zinv[:, wi, grp * 4:grp * 4 + 4].rearrange(
                        "p (h v) -> p h v", v=1).broadcast_to([128, 4, 128])
                    return eng.tensor_tensor(dbuf[:, pair], iv, zz, ALU.mult)
                sch.add("dve", t_dbuild, reads=(r_zi, r_c),
                        writes=(r_d[pair],))

            def wg_tail(si, qj, wi, w, nw, grp):
                pair = si % 2

                def t_tr(eng, pair=pair):
                    last = None
                    for i in range(8):
                        hi = i // 2
                        last = eng.matmul(
                            PING[pair][:, i * 128:(i + 1) * 128],
                            pbuf[:, pair, i * 128:(i + 1) * 128],
                            dbuf[:, pair, hi, :], start=True, stop=True)
                    return last
                sch.add("pe", t_tr, reads=(r_p[pair], r_d[pair]),
                        writes=(r_sb[2 * pair], r_sb[2 * pair + 1]))

                def t_pteA(eng, pair=pair, wi=wi):
                    return eng.copy(scr[:, wi * 1024:wi * 1024 + 512],
                                    PING[pair][:, 0:512])
                sch.add("act", t_pteA,
                        reads=(r_sb[2 * pair],),
                        writes=(r_scr[2 * wi],))

                def t_pteB(eng, pair=pair, wi=wi):
                    return eng.tensor_copy(
                        scr[:, wi * 1024 + 512:(wi + 1) * 1024],
                        PING[pair][:, 512:1024])
                sch.add("dve", t_pteB,
                        reads=(r_sb[2 * pair + 1],),
                        writes=(r_scr[2 * wi + 1],))

                if wi != nw - 1:
                    return
                wins = [w2 for w2 in (2 * sb + qj - 1, 2 * sb + qj)
                        if 0 <= w2 <= 30]
                vres = tuple({r_v[w2 % 6] for w2 in wins} |
                             {r_v[(w2 + 1) % 6] for w2 in wins})

                def t_av(eng, grp=grp, wins=wins):
                    last = None
                    for hi in range(4):
                        h = grp * 4 + hi
                        pb = 64 * (h % 2)
                        nmm = 2 * len(wins)
                        i_mm = 0
                        for wi2, w2 in enumerate(wins):
                            for kbi, kb in enumerate((w2, w2 + 1)):
                                last = eng.matmul(
                                    CTXPS[pb:pb + 64,
                                          (h // 2) * 128:
                                          (h // 2) * 128 + 128],
                                    vr[:, kb % 6, h * 64:(h + 1) * 64],
                                    scr[:, wi2 * 1024 + hi * 256 + kbi * 128:
                                        wi2 * 1024 + hi * 256 +
                                        kbi * 128 + 128],
                                    start=(i_mm == 0), stop=(i_mm == nmm - 1),
                                    tile_position=(0, pb))
                                i_mm += 1
                    return last
                sch.add("pe", t_av, reads=vres + tuple(r_scr),
                        writes=(r_cx[grp // 2],))

                sc = 0.5 if len(wins) == 2 else 1.0

                def t_ctxe(eng, qj=qj, grp=grp, sc=sc):
                    return eng.activation(
                        ctx[:, 2 * grp:2 * grp + 2,
                            qj * 128:(qj + 1) * 128],
                        CTXPS[:, grp * 256:(grp + 1) * 256].rearrange(
                            "p (c q) -> p c q", c=2),
                        AF.Copy, scale=sc)
                sch.add("act", t_ctxe, reads=(r_cx[grp // 2], r_c),
                        writes=(r_ctx[grp // 2],))

            stream = [(qj, wi, w, nw, grp)
                      for qj, wi, w, nw in wgs for grp in range(4)]
            stream = []
            for qj, qwins in enumerate(QJW2 := [
                    [w for w in (2 * sb + qj2 - 1, 2 * sb + qj2)
                     if 0 <= w <= 30] for qj2 in range(2)]):
                for grp in range(4):
                    for wi, w in enumerate(qwins):
                        stream.append((qj, wi, w, len(qwins), grp))
            hook_si = next((si for si, wg in enumerate(stream)
                            if wg[0] == 1), None)
            if not (g < NSB and 2 * sb + 1 <= 30):
                hook_si = None
            pending = []
            stream = stream[:NWG]
            for si, wg in enumerate(stream):
                if si == hook_si and not NOHOOK:
                    while pending:
                        wg_tail(*pending.pop(0))
                    emit_ksq(g)
                    emit_negmh(sb, 1, 1)
                if len(pending) >= 2:
                    wg_tail(*pending.pop(0))
                wg_front(si, *wg)
                pending.append((si,) + wg)
                if not NOKV:
                    pop_kv(1)
            while pending:
                wg_tail(*pending.pop(0))
            pop_kv(2)

            if STOP_AFTER == "attn":
                continue
            for p in range(4):
                bank = p % 2

                def t_womm(eng, p=p, bank=bank):
                    last = None
                    for o in range(2):
                        oc = 2 * p + o
                        dst = B45[bank][:, o * TSB:(o + 1) * TSB]
                        for kc in range(NC):
                            last = eng.matmul(
                                dst, wo[:, kc, oc * 128:(oc + 1) * 128],
                                ctx[:, kc, :],
                                start=(kc == 0), stop=(kc == NC - 1))
                    return last
                sch.add("pe", t_womm, reads=(r_w,) + tuple(r_ctx),
                        writes=(RB45[bank],))

                def t_bre(eng, p=p, bank=bank):
                    return eng.tensor_copy(
                        xbv[:, 2 * p:2 * p + 2, :],
                        B45[bank][:, :].rearrange("p (c t) -> p c t", c=2))
                sch.add("dve", t_bre, reads=(RB45[bank],), writes=(r_xbq,))

            if STOP_AFTER == "wo":
                continue
            def t_tall(eng):
                rb = rbsb[:].rearrange("p (u t) -> p u t", u=1).broadcast_to(
                    [128, NC, TSB])
                return eng.tensor_tensor(
                    scr[:].rearrange("p (c t) -> p c t", c=NC),
                    xbv, rb, ALU.mult)

            def t_resid(eng, sl=sl):
                return eng.tensor_tensor(
                    xr[:, sl], xr[:, sl],
                    scr[:].rearrange("p (c t) -> p c t", c=NC), ALU.add)

            def t_residm(eng, sl=sl):
                return eng.tensor_tensor(xr[:, sl], xr[:, sl], negmr_bc,
                                         ALU.add)

            emit_ln_stats(with_negm=True)
            while v_defer:
                v_defer.pop(0)()
            sch.add("dve", t_tall, reads=(r_xbq, r_rb), writes=tuple(r_scr))
            sch.add("dve", t_resid, reads=(r_x[sl],) + tuple(r_scr),
                    writes=(r_x[sl],))
            sch.add("dve", t_residm, reads=(r_x[sl], r_s5),
                    writes=(r_x[sl],))

            if STOP_AFTER == "ln2":
                continue
            def t_cast2(eng, sl=sl):
                return eng.tensor_copy(xbv, xr[:, sl])
            sch.add("dve", t_cast2, reads=(r_x[sl],), writes=(r_xbq,))
            emit_ln_stats(with_negm=False, mean_dst=negmr3, evac_rb=True)

            if STOP_AFTER == "ln3":
                continue
            def u1e_fn(eng):
                rb = rbsb[:].rearrange("p (u t) -> p u t", u=1).broadcast_to(
                    [128, 2, TSB])
                return eng.tensor_tensor(
                    p2sb[:], B45[0][:, :].rearrange("p (c t) -> p c t", c=2),
                    rb, ALU.mult)
            proj_pair_unit(u1, cu1row, 0, "dve", u1e_fn, (r_p2sb,),
                           xsrc=xbv, negrow=negmr3,
                           rx=(r_xbq,), evac_reads=(r_rb,), rneg=r_mean)()

            u2_first = [None]

            def emit_v1(grp):
                ping = grp % 2

                def t_v1mm(eng, grp=grp, ping=ping):
                    last = None
                    for hc in range(4):
                        ffc = grp * 4 + hc
                        dst = PING[ping][:, hc * 256:(hc + 1) * 256]
                        for kc in range(2):
                            last = eng.matmul(
                                dst, v1[:, kc, ffc * 128:(ffc + 1) * 128],
                                p2sb[:, kc, :], start=(kc == 0),
                                stop=(kc == 1))
                    return last
                sch.add("pe", t_v1mm, reads=(r_w, r_p2sb),
                        writes=(r_sb[2 * ping], r_sb[2 * ping + 1]))

                def t_gelu(eng, grp=grp, ping=ping):
                    last = None
                    for hc in range(4):
                        ffc = grp * 4 + hc
                        last = eng.activation(
                            ctx[:, (grp % 2) * 4 + hc, :],
                            PING[ping][:, hc * 256:(hc + 1) * 256],
                            AF.Gelu_apprx_tanh,
                            bias=b1col[:, ffc:ffc + 1])
                    return last
                sch.add("act", t_gelu,
                        reads=(r_sb[2 * ping], r_sb[2 * ping + 1], r_w),
                        writes=(r_ctx[grp % 2],))

            def emit_u2(grp):
                def t_u2mm(eng, grp=grp):
                    last = None
                    for oc in range(2):
                        dst = (P2[:, 1024:1280] if oc == 0
                               else P2[:, 1536:1792])
                        for hc in range(4):
                            ffc = grp * 4 + hc
                            last = eng.matmul(
                                dst, u2[:, ffc, oc * 128:(oc + 1) * 128],
                                ctx[:, (grp % 2) * 4 + hc, :],
                                start=(grp == 0 and hc == 0),
                                stop=(grp == 7 and hc == 3))
                    return last
                if grp == 0:
                    u2_first[0] = sch.add("pe", t_u2mm,
                                          reads=(r_w, r_ctx[0]),
                                          writes=(r_cx[0], r_cx[1]))
                else:
                    sch.add("pe", t_u2mm, reads=(r_w, r_ctx[grp % 2]),
                            extra=(u2_first[0],))

            for grp in range(8):
                emit_v1(grp)
                if grp >= 1:
                    emit_u2(grp - 1)
                if grp % 2 == 1:
                    pop_q(1)
            emit_u2(7)
            r_cx[0].w = len(sch.tasks) - 1
            r_cx[1].w = len(sch.tasks) - 1
            if g < NSB:
                emit_qside_sq(g)

            def t_u2e(eng):
                eng.tensor_copy(q2sb[:, 0, :], P2[:, 1024:1280])
                return eng.tensor_copy(q2sb[:, 1, :], P2[:, 1536:1792])
            sch.add("dve", t_u2e, reads=(r_cx[0], r_cx[1]), writes=(r_q2sb,))

            for p in range(4):
                bank = p % 2

                def t_v2mm(eng, p=p, bank=bank):
                    last = None
                    for o in range(2):
                        oc = 2 * p + o
                        dst = B45[bank][:, o * TSB:(o + 1) * TSB]
                        for kc in range(2):
                            last = eng.matmul(
                                dst, v2[:, kc, oc * 128:(oc + 1) * 128],
                                q2sb[:, kc, :], start=(kc == 0),
                                stop=(kc == 1))
                    return last
                sch.add("pe", t_v2mm, reads=(r_w, r_q2sb),
                        writes=(RB45[bank],))

                def t_fbe(eng, p=p, bank=bank):
                    last = None
                    for o in range(2):
                        oc = 2 * p + o
                        last = eng.tensor_scalar(
                            xbv[:, oc, :],
                            B45[bank][:, o * TSB:(o + 1) * TSB],
                            b2col[:, oc:oc + 1], 0.0, ALU.add, ALU.add)
                    return last
                sch.add("dve", t_fbe, reads=(RB45[bank], r_w),
                        writes=(r_xbq,))

            if g < NSB:
                emit_qside(g)
                wins0 = [w for w in (2 * g - 1, 2 * g) if 0 <= w <= 30]
                for wi in range(len(wins0)):
                    emit_negmh(g, 0, wi)
                emit_negmh(g, 1, 0)

            emit_ln_stats(with_negm=True)
            sch.add("dve", t_tall, reads=(r_xbq, r_rb), writes=tuple(r_scr))
            sch.add("dve", t_resid, reads=(r_x[sl],) + tuple(r_scr),
                    writes=(r_x[sl],))
            sch.add("dve", t_residm, reads=(r_x[sl], r_s5),
                    writes=(r_x[sl],))

            dma_sync(xr[:, sl], out_re[:, :, sb * TSB:(sb + 1) * TSB],
                     reads=(r_x[sl],), writes=(), chan=f"st{sb % 3}")

        pop_kv(len(kv_units))
        while v_defer:
            v_defer.pop(0)()
        pop_q(len(q_units))

        if g == 0:
            emit_qside_sq(0)
            emit_qside(0)
            emit_ksq(0)
            wins0 = [w for w in (-1, 0) if 0 <= w <= 30]
            for wi in range(len(wins0)):
                emit_negmh(0, 0, wi)
            emit_negmh(0, 1, 0)

    return nc, sch


def emit_program(nc, sch):
    counts = {e: 0 for e in Sched.ENGINES}
    dma_counts = {}
    for t in sch.tasks:
        if t.is_dma:
            dma_counts[t.chan] = dma_counts.get(t.chan, 0) + 16
            t.sem_val = dma_counts[t.chan]
        else:
            counts[t.eng] += 1
            t.sem_val = counts[t.eng]

    with contextlib.ExitStack() as st:
        sems = {e: st.enter_context(nc.semaphore(f"sem_{e}"))
                for e in Sched.ENGINES}
        dsems = {ch: st.enter_context(nc.semaphore(f"dma_{ch}"))
                 for ch in dma_counts}
        block = st.enter_context(nc.Block())

        def emit_engine(eng_name, eng):
            best = {}
            for t in sch.tasks:
                if t.eng != eng_name:
                    continue
                for d in sorted(t.deps):
                    dt = sch.tasks[d]
                    if dt.is_dma:
                        key = ("dma", dt.chan)
                        sem = dsems[dt.chan]
                    else:
                        key = dt.eng
                        sem = sems[dt.eng]
                    v = dt.sem_val
                    if best.get(key, 0) >= v:
                        continue
                    eng.wait_ge(sem, v)
                    best[key] = v
                last = t.emit(eng)
                assert last is not None, t.tid
                if t.is_dma:
                    last.then_inc(dsems[t.chan], 16)
                else:
                    last.then_inc(sems[t.eng], 1)
            if eng_name == "sync":
                for ch in ("st0", "st1", "st2"):
                    if ch in dma_counts:
                        eng.wait_ge(dsems[ch], dma_counts[ch])

        @block.tensor
        def _(eng):
            emit_engine("pe", eng)

        @block.scalar
        def _(eng):
            emit_engine("act", eng)

        @block.vector
        def _(eng):
            emit_engine("dve", eng)

        @block.gpsimd
        def _(eng):
            emit_engine("pool", eng)

        @block.sync
        def _(eng):
            emit_engine("sync", eng)
    return nc


_CACHED = {}


def _get_nc():
    if "nc" not in _CACHED:
        nc, sch = build_nc()
        emit_program(nc, sch)
        _CACHED["nc"] = nc
    return _CACHED["nc"]


def kernel(L0_attn_hidden_states, L0_attn_Wq, L0_attn_Wk, L0_attn_Wv,
           L0_attn_Wo, L0_ffn_U1, L0_ffn_V1, L0_ffn_b1, L0_ffn_U2,
           L0_ffn_V2, L0_ffn_b2):
    x = np.asarray(L0_attn_hidden_states, dtype=np.float32)
    weights = {
        "wq": np.ascontiguousarray(np.asarray(L0_attn_Wq, np.float32)),
        "wk": np.ascontiguousarray(np.asarray(L0_attn_Wk, np.float32)),
        "wv": np.ascontiguousarray(np.asarray(L0_attn_Wv, np.float32)),
        "wo": np.ascontiguousarray(np.asarray(L0_attn_Wo, np.float32)),
        "u1": np.ascontiguousarray(np.asarray(L0_ffn_U1, np.float32)),
        "v1": np.ascontiguousarray(np.asarray(L0_ffn_V1, np.float32)),
        "b1": np.ascontiguousarray(np.asarray(L0_ffn_b1, np.float32)),
        "u2": np.ascontiguousarray(np.asarray(L0_ffn_U2, np.float32)),
        "v2": np.ascontiguousarray(np.asarray(L0_ffn_V2, np.float32)),
        "b2": np.ascontiguousarray(np.asarray(L0_ffn_b2, np.float32)),
    }
    nc = _get_nc()
    in_maps = []
    for b in range(8):
        m = dict(weights)
        m["x"] = np.ascontiguousarray(x[b].T)
        in_maps.append(m)
    res = run_bass_kernel_spmd(nc, in_maps, core_ids=list(range(8)))
    out = np.empty((8, S, D), np.float32)
    for b in range(8):
        out[b] = res.results[b]["out"].T
    return out



# version 17
# speedup vs baseline: 1.0514x; 1.0514x over previous
import contextlib

import numpy as np

import concourse.bass as bass
import concourse.mybir as mybir
from concourse.bass_utils import run_bass_kernel_spmd

F32 = mybir.dt.float32
BF16 = mybir.dt.bfloat16
AF = mybir.ActivationFunctionType
ALU = mybir.AluOpType
AX = mybir.AxisListType

S = 4096
D = 1024
NH = 16
DH = 64
TSB = 256
NSB = 16
DFF = 4096
RANK = 256
NC = 8

R_1024 = 1.0 / 1024.0
STOP_AFTER = None
GRPS = (0, 1, 2, 3)
NOHOOK = False
NOKV = False
NWG = 99


class Res:
    __slots__ = ("w", "rs", "name")

    def __init__(self, name=""):
        self.w = None
        self.rs = []
        self.name = name


class Task:
    __slots__ = ("tid", "eng", "emit", "deps", "is_dma", "chan", "sem_val")

    def __init__(self, tid, eng, emit, deps, is_dma, chan=None):
        self.tid = tid
        self.eng = eng
        self.emit = emit
        self.deps = deps
        self.is_dma = is_dma
        self.chan = chan
        self.sem_val = None


class Sched:
    ENGINES = ("pe", "act", "dve", "pool", "sync")

    def __init__(self):
        self.tasks = []

    def add(self, eng, emit, reads=(), writes=(), is_dma=False, extra=(),
            chan=None):
        deps = set(extra)
        for r in reads:
            if r.w is not None:
                deps.add(r.w)
        for r in writes:
            if r.w is not None:
                deps.add(r.w)
            deps.update(r.rs)
        t = Task(len(self.tasks), eng, emit, deps, is_dma, chan)
        self.tasks.append(t)
        for r in reads:
            r.rs.append(t.tid)
        for r in writes:
            r.w = t.tid
            r.rs = []
        return t.tid


def build_nc(n_g=NSB + 1):
    nc = bass.Bass("TRN2", target_bir_lowering=False)

    x_d = nc.dram_tensor("x", [D, S], F32, kind="ExternalInput").ap()
    out_d = nc.dram_tensor("out", [D, S], F32, kind="ExternalOutput").ap()
    wq_d = nc.dram_tensor("wq", [D, D], F32, kind="ExternalInput").ap()
    wk_d = nc.dram_tensor("wk", [D, D], F32, kind="ExternalInput").ap()
    wv_d = nc.dram_tensor("wv", [D, D], F32, kind="ExternalInput").ap()
    wo_d = nc.dram_tensor("wo", [D, D], F32, kind="ExternalInput").ap()
    u1_d = nc.dram_tensor("u1", [D, RANK], F32, kind="ExternalInput").ap()
    v1_d = nc.dram_tensor("v1", [RANK, DFF], F32, kind="ExternalInput").ap()
    b1_d = nc.dram_tensor("b1", [DFF], F32, kind="ExternalInput").ap()
    u2_d = nc.dram_tensor("u2", [DFF, RANK], F32, kind="ExternalInput").ap()
    v2_d = nc.dram_tensor("v2", [RANK, D], F32, kind="ExternalInput").ap()
    b2_d = nc.dram_tensor("b2", [D], F32, kind="ExternalInput").ap()

    A = nc.alloc_sbuf_tensor
    wq = A("wq_sb", [128, NC, D], BF16)
    wk = A("wk_sb", [128, NC, D], BF16)
    wv = A("wv_sb", [128, NC, D], BF16)
    wo = A("wo_sb", [128, NC, D], BF16)
    u1 = A("u1_sb", [128, NC, RANK], BF16)
    v1 = A("v1_sb", [128, 2, DFF], BF16)
    u2 = A("u2_sb", [128, 32, RANK], BF16)
    v2 = A("v2_sb", [128, 2, D], BF16)
    b1col = A("b1col", [128, 32], F32)
    b2col = A("b2col", [128, 8], F32)
    cqrow = A("cqrow", [1, D], BF16)
    ckrow = A("ckrow", [1, D], BF16)
    cvrow = A("cvrow", [1, D], BF16)
    cu1row = A("cu1row", [1, RANK], BF16)

    ones_col = A("ones_col", [128, 1], BF16)
    ones_f32 = A("ones_f32", [1, 128], F32)
    ident = A("ident", [128, 128], BF16)
    cst = A("cst", [128, 8], F32)
    negmh = A("negmh", [128, 16], F32)
    kcmaxr = A("kcmaxr", [1, 6, 8], F32)
    qcln = A("qcln", [128, 16], F32)
    mrow = A("mrow", [1, 64], F32)

    xr = A("x_ring", [128, 3, NC, TSB], F32)
    xbsq = A("xbsq", [128, NC, 512], BF16)
    xhat = A("xhat", [128, NC, TSB], BF16)
    negmr3 = A("negmr3_row", [1, TSB], BF16)
    scr = A("scr", [128, 2048], BF16)
    qr = A("q_ring", [128, 2, NC, TSB], BF16)
    kr = A("k_ring", [128, NC, 6, 128], BF16)
    vr = A("v_ring", [128, 6, D], BF16)
    pbuf = A("pbuf", [128, 2, 1024], BF16)
    x1hat = pbuf[:].rearrange("p a t -> p (a t)").rearrange(
        "p (c t) -> p c t", c=NC)
    dbuf = A("dbuf", [128, 2, 4, 128], BF16)
    zbuf = A("zbuf", [128, 2, NH], F32)
    zinv = A("zinv", [128, 2, NH], BF16)
    ctx = A("ctx_sb", [128, NC, TSB], BF16)
    p2sb = A("p2sb", [128, 2, TSB], BF16)
    q2sb = A("q2sb", [128, 2, TSB], BF16)
    rbsb = A("rb_sb", [128, TSB], F32)
    stats = A("stats_sb", [1, 512], F32)
    small = A("small_sb", [1, 768], F32)
    rstd = A("rstd_row", [1, TSB], F32)
    negmr = A("negmr_row", [1, TSB], BF16)
    negmrf = A("negmrf_row", [1, TSB], F32)

    P1 = nc.alloc_psum_tensor("P1", [128, 2048], F32)
    P2 = nc.alloc_psum_tensor("P2", [128, 2048], F32)

    CONSTS = {0: 0.125, 1: 0.5, 2: R_1024, 3: -R_1024, 4: 1e-5, 5: -0.5}
    for j, v in CONSTS.items():
        nc.const_aps.register(F32, float(v), cst[:, j:j + 1])

    sch = Sched()
    r_x = [Res(f"x{i}") for i in range(3)]
    r_xbq = Res("xbsq")
    r_xhat = Res("xhat")
    r_scr = [Res(f"scr{i}") for i in range(4)]
    r_q = [Res("q0"), Res("q1")]
    r_k = [Res(f"k{i}") for i in range(6)]
    r_v = [Res(f"v{i}") for i in range(6)]
    r_p = [Res(f"pbuf{i}") for i in range(4)]
    r_d = [Res("dbufA"), Res("dbufB")]
    r_z = Res("zbuf")
    r_zi = Res("zinv")
    r_nm = [Res(f"nm{i}") for i in range(16)]
    r_mh = Res("mhat")
    r_ctx = [Res("ctxA"), Res("ctxB")]
    r_p2sb = Res("p2sb")
    r_q2sb = Res("q2sb")
    r_rb = Res("rbsb")
    r_st = Res("stats")
    r_ln = Res("lnrows")
    r_mean = Res("meanrow")
    r_sb = [Res(f"P1b{i}") for i in range(4)]
    r_pA = r_sb[0]
    r_pB = r_sb[2]
    r_s4 = Res("P2bank4")
    r_s5 = Res("P2bank5")
    r_cx = [Res("P2ctx6"), Res("P2ctx7")]
    r_w = Res("weights")
    r_c = Res("consts")

    PING = [P1[:, 0:1024], P1[:, 1024:2048]]
    SBANK = [P1[:, i * 512:(i + 1) * 512] for i in range(4)]
    SCRS = [scr[:, i * 512:(i + 1) * 512] for i in range(4)]
    RPING = [r_pA, r_pB]
    B45 = [P2[:, 0:512], P2[:, 512:1024]]
    RB45 = [r_s4, r_s5]
    CTXPS = P2[:, 1024:2048]

    def t_init_memsets(eng):
        eng.memset(negmh[:], 0.0)
        eng.memset(kcmaxr[:], 1.0)
        eng.memset(qcln[:], 0.0)
        eng.memset(ones_col[:], 1.0)
        eng.memset(ones_f32[:], 1.0)
        last = None
        for j, v in CONSTS.items():
            last = eng.memset(cst[:, j:j + 1], float(v))
        return last
    sch.add("dve", t_init_memsets, writes=(r_c,))

    iota_scr = pbuf[:].rearrange("p a t -> p (a t)").bitcast(F32)

    def t_iota(eng):
        eng.iota(iota_scr[:, 0:128], pattern=[[0, 128]], channel_multiplier=1,
                 allow_small_or_imprecise_dtypes=True)
        return eng.iota(iota_scr[:, 128:256], pattern=[[1, 128]],
                        channel_multiplier=0,
                        allow_small_or_imprecise_dtypes=True)
    sch.add("pool", t_iota, writes=tuple(r_p))

    def t_ident(eng):
        return eng.tensor_tensor(ident[:], iota_scr[:, 0:128],
                                 iota_scr[:, 128:256], ALU.is_equal)
    sch.add("dve", t_ident, reads=tuple(r_p), writes=(r_c,))

    for wi_, (dst, src) in enumerate((
        (wq, wq_d.rearrange("(kc p) j -> p kc j", p=128)),
        (wk, wk_d.rearrange("(kc p) j -> p kc j", p=128)),
        (wv, wv_d.rearrange("(kc p) j -> p kc j", p=128)),
        (wo, wo_d.rearrange("(kc p) j -> p kc j", p=128)),
        (u1, u1_d.rearrange("(kc p) j -> p kc j", p=128)),
        (v1, v1_d.rearrange("(kc p) j -> p kc j", p=128)),
        (u2, u2_d.rearrange("(kc p) j -> p kc j", p=128)),
        (v2, v2_d.rearrange("(kc p) j -> p kc j", p=128)),
    )):
        sch.add("pool",
                (lambda d, s: lambda eng: eng.dma_start(d[:], s))(dst, src),
                writes=(r_w,), is_dma=True, chan=f"w{wi_}")

    b1stage = pbuf[:].rearrange("p a t -> p (a t)")[0:32, 0:128]
    b2stage = pbuf[:].rearrange("p a t -> p (a t)")[0:8, 128:256]
    sch.add("pool",
            lambda eng: eng.dma_start(b1stage,
                                      b1_d.rearrange("(c p) -> c p", p=128)),
            writes=tuple(r_p), is_dma=True, chan="w8")
    sch.add("pool",
            lambda eng: eng.dma_start(b2stage,
                                      b2_d.rearrange("(c p) -> c p", p=128)),
            writes=tuple(r_p), is_dma=True, chan="w9")

    def t_btr(eng):
        eng.matmul(P2[:, 0:32], b1stage, ident[0:32, 0:32],
                   start=True, stop=True)
        return eng.matmul(P2[:, 32:40], b2stage, ident[0:8, 0:8],
                          start=True, stop=True)
    sch.add("pe", t_btr, reads=tuple(r_p) + (r_c,), writes=(r_s4,))

    def t_bcol(eng):
        eng.tensor_copy(b1col[:], P2[:, 0:32])
        return eng.tensor_copy(b2col[:], P2[:, 32:40])
    sch.add("dve", t_bcol, reads=(r_s4,), writes=(r_w,))

    def t_crows_a(eng):
        last = None
        for wsb, base in ((wq, 0), (wk, 1024)):
            for half in range(2):
                for kc in range(NC):
                    last = eng.matmul(
                        P1[0:1, base + half * 512: base + (half + 1) * 512],
                        ones_col[:], wsb[:, kc, half * 512:(half + 1) * 512],
                        start=(kc == 0), stop=(kc == NC - 1))
        return last
    sch.add("pe", t_crows_a, reads=(r_w, r_c), writes=tuple(r_sb))

    def t_crows_a_evac(eng):
        eng.tensor_copy(cqrow[0:1, :], P1[0:1, 0:1024])
        return eng.tensor_copy(ckrow[0:1, :], P1[0:1, 1024:2048])
    sch.add("dve", t_crows_a_evac, reads=tuple(r_sb), writes=(r_w,))

    def t_crows_b(eng):
        last = None
        for half in range(2):
            for kc in range(NC):
                last = eng.matmul(P1[0:1, half * 512:(half + 1) * 512],
                                  ones_col[:],
                                  wv[:, kc, half * 512:(half + 1) * 512],
                                  start=(kc == 0), stop=(kc == NC - 1))
        for kc in range(NC):
            last = eng.matmul(P1[0:1, 1024:1280], ones_col[:], u1[:, kc, :],
                              start=(kc == 0), stop=(kc == NC - 1))
        return last
    sch.add("pe", t_crows_b, reads=(r_w, r_c) + tuple(r_sb), writes=tuple(r_sb))

    def t_crows_b_evac(eng):
        eng.tensor_copy(cvrow[0:1, :], P1[0:1, 0:1024])
        return eng.tensor_copy(cu1row[0:1, :], P1[0:1, 1024:1280])
    sch.add("dve", t_crows_b_evac, reads=tuple(r_sb), writes=(r_w,))

    rb_bc = P2[:, 512:768].rearrange("p (u t) -> p u t", u=1).broadcast_to(
        [128, NC, TSB])
    negmr_bc = P2[:, 768:1024].rearrange(
        "p (u t) -> p u t", u=1).broadcast_to([128, NC, TSB])
    xbv = xbsq[:].rearrange("p c (a t) -> p c a t", a=2)[:, :, 0, :]
    sqv_t = xbsq[:].rearrange("p c (a t) -> p c a t", a=2)[:, :, 1, :]

    def emit_ln_stats(with_negm, pre_negmr=None, negmr_dst=None,
                      mean_dst=None, evac_rb=False):
        def t_sq(eng):
            return eng.tensor_tensor(sqv_t, xbv, xbv, ALU.mult)
        sch.add("dve", t_sq, reads=(r_xbq,), writes=(r_xbq,))

        def t_sums(eng):
            last = None
            for c in range(NC):
                last = eng.matmul(P2[0:1, 0:512], ones_col[:], xbsq[:, c, :],
                                  start=(c == 0), stop=(c == NC - 1))
            return last
        sch.add("pe", t_sums, reads=(r_xbq, r_c), writes=(r_s4,))

        def t_stats_evac(eng):
            return eng.tensor_copy(stats[0:1, :], P2[0:1, 0:512])
        sch.add("dve", t_stats_evac, reads=(r_s4,), writes=(r_st,))

        sumv = stats[0:1, 0:TSB]
        sqv = stats[0:1, TSB:512]

        if mean_dst is not None:
            def t_mean(eng):
                return eng.tensor_single_scalar(mean_dst[0:1, :], sumv,
                                                -R_1024, ALU.mult)
            sch.add("dve", t_mean, reads=(r_st, r_c), writes=(r_mean,))

        def t_small_dve1(eng):
            return eng.scalar_tensor_tensor(small[0:1, 0:TSB], sumv, R_1024,
                                            sumv, ALU.mult, ALU.mult)
        sch.add("dve", t_small_dve1, reads=(r_st, r_c), writes=(r_st,))

        def t_small_dve2(eng):
            return eng.tensor_tensor(small[0:1, TSB:512], sqv,
                                     small[0:1, 0:TSB], ALU.subtract)
        sch.add("dve", t_small_dve2, reads=(r_st,), writes=(r_st,))

        def t_small_act1(eng):
            return eng.activation(small[0:1, 512:768], small[0:1, TSB:512],
                                  AF.Ln, scale=R_1024, bias=1e-5)
        sch.add("act", t_small_act1, reads=(r_st, r_c), writes=(r_st,))

        def t_small_act2(eng):
            return eng.activation(rstd[0:1, :], small[0:1, 512:768], AF.Exp,
                                  scale=-0.5)
        sch.add("act", t_small_act2, reads=(r_st, r_c), writes=(r_st, r_ln))

        if pre_negmr is not None:
            pre_negmr()

        if mean_dst is None:
            def t_negmr(eng):
                dst = negmrf[0:1, :] if with_negm else (
                    negmr_dst if negmr_dst is not None else negmr)[0:1, :]
                return eng.scalar_tensor_tensor(dst, sumv, -R_1024,
                                                rstd[0:1, :], ALU.mult,
                                                ALU.mult)
            sch.add("dve", t_negmr, reads=(r_st, r_ln, r_c), writes=(r_ln,))

        def t_bcast(eng):
            last = eng.matmul(P2[:, 512:768], ones_f32[:], rstd[0:1, :],
                              start=True, stop=True)
            if with_negm:
                last = eng.matmul(P2[:, 768:1024], ones_f32[:], negmrf[0:1, :],
                                  start=True, stop=True)
            return last
        sch.add("pe", t_bcast, reads=(r_ln, r_c), writes=(r_s5,))

        if with_negm or evac_rb:
            def t_rb_evac(eng):
                return eng.tensor_copy(rbsb[:], P2[:, 512:768])
            sch.add("dve", t_rb_evac, reads=(r_s5,), writes=(r_rb,))

    def proj_pair_unit(wsb, crow, p, evac_eng, evac_fn, evac_writes,
                       xsrc=None, negrow=None, rx=None, evac_reads=(),
                       rneg=None):
        bank = p % 2
        xsrc = xhat if xsrc is None else xsrc
        negrow = negmr if negrow is None else negrow
        rx = (r_xhat,) if rx is None else rx
        rneg = r_ln if rneg is None else rneg

        def t_mm(eng, p=p, bank=bank, wsb=wsb, crow=crow):
            last = None
            for o in range(2):
                oc = 2 * p + o
                dst = B45[bank][:, o * TSB:(o + 1) * TSB]
                for kc in range(NC):
                    last = eng.matmul(dst,
                                      wsb[:, kc, oc * 128:(oc + 1) * 128],
                                      xsrc[:, kc, :],
                                      start=(kc == 0), stop=False)
                last = eng.matmul(dst, crow[0:1, oc * 128:(oc + 1) * 128],
                                  negrow[0:1, :], start=False, stop=True)
            return last

        def emit():
            sch.add("pe", t_mm, reads=(r_w, rneg) + tuple(rx),
                    writes=(RB45[bank],))
            sch.add(evac_eng, evac_fn, reads=(RB45[bank],) + tuple(evac_reads),
                    writes=tuple(evac_writes))
        return emit

    r_kc = [Res(f"kc{i}") for i in range(6)]
    mh_scr = nc.alloc_sbuf_tensor("mh_scr", [128, 8], F32)

    def emit_ksq(g):
        s0 = (2 * g) % 6

        def t_ksq(eng, s0=s0):
            kv = kr[:, :, s0:s0 + 2, :]
            sv = scr[:].rearrange("p (c b q) -> p c b q", c=NC, b=2)
            return eng.tensor_tensor(sv, kv, kv, ALU.mult)
        sch.add("pool", t_ksq, reads=(r_k[s0], r_k[s0 + 1]),
                writes=tuple(r_scr))

        for b in range(2):
            def t_kcmm(eng, b=b):
                last = None
                sv = scr[:].rearrange("p (c b q) -> p c b q", c=NC, b=2)
                for c in range(NC):
                    last = eng.matmul(P2[0:1, c * 128:(c + 1) * 128],
                                      ones_col[:], sv[:, c, b, :],
                                      start=True, stop=True)
                return last
            sch.add("pe", t_kcmm, reads=tuple(r_scr) + (r_c,),
                    writes=(r_s4, r_s5))

            def t_kcred(eng, b=b, s0=s0):
                return eng.tensor_reduce(
                    kcmaxr[0:1, s0 + b, :].rearrange("a (c v) -> a c v", v=1),
                    P2[0:1, 0:1024].rearrange("a (c q) -> a c q", c=NC),
                    axis=AX.X, op=ALU.max)
            sch.add("dve", t_kcred, reads=(r_s4, r_s5),
                    writes=(r_kc[s0 + b],))

    def emit_negmh(sbx, qj, wi):
        qbx = 2 * sbx + qj
        wlist = [w for w in (qbx - 1, qbx) if 0 <= w <= 30]
        w = wlist[wi]
        idx = qj * 2 + wi

        def t_kw(eng, w=w, idx=idx):
            return eng.tensor_tensor(mrow[0:1, idx * 8:(idx + 1) * 8],
                                     kcmaxr[0:1, w % 6, :],
                                     kcmaxr[0:1, (w + 1) % 6, :], ALU.max)
        sch.add("dve", t_kw, reads=(r_kc[w % 6], r_kc[(w + 1) % 6]),
                writes=(r_mh,))

        def t_lnkw(eng, idx=idx):
            return eng.activation(mrow[0:1, 32 + idx * 8:40 + idx * 8],
                                  mrow[0:1, idx * 8:(idx + 1) * 8], AF.Ln)
        sch.add("act", t_lnkw, reads=(r_mh,), writes=(r_mh,))

        def t_bc(eng, idx=idx):
            return eng.matmul(P2[:, 1016:1024], ones_f32[:],
                              mrow[0:1, 32 + idx * 8:40 + idx * 8],
                              start=True, stop=True)
        sch.add("pe", t_bc, reads=(r_mh, r_c), writes=(r_s5,))

        def t_add(eng, qj=qj):
            return eng.tensor_tensor(mh_scr[:], qcln[:, qj * 8:qj * 8 + 8],
                                     P2[:, 1016:1024], ALU.add)
        sch.add("dve", t_add, reads=(r_mh, r_s5), writes=(r_mh,))

        def t_exp2(eng):
            return eng.activation(mh_scr[:], mh_scr[:], AF.Exp, scale=0.5)
        sch.add("act", t_exp2, reads=(r_mh, r_c), writes=(r_mh,))

        def t_red(eng, qj=qj, wi=wi):
            return eng.reduce_max(
                negmh[:, qj * 8 + wi * 4:qj * 8 + wi * 4 + 4].rearrange(
                    "p (g v) -> p g v", v=1),
                mh_scr[:].rearrange("p (g c) -> p g c", g=4),
                axis=AX.X, negate=True)
        nres = tuple(r_nm[qj * 8 + wi * 4 + grp] for grp in range(4))
        sch.add("dve", t_red, reads=(r_mh,), writes=nres + (r_mh,))

    def emit_qside_sq(g):
        qsl_ = g % 2

        def t_qsq(eng, qsl_=qsl_):
            pv = pbuf[:].rearrange("p a t -> p (a t)").rearrange(
                "p (c q) -> p c q", c=NC)
            return eng.tensor_tensor(pv, qr[:, qsl_], qr[:, qsl_], ALU.mult)
        sch.add("pool", t_qsq, reads=(r_q[qsl_],), writes=(r_p[0], r_p[1]))

    def emit_qside(g):
        def t_qn(eng):
            last = None
            pv = pbuf[:].rearrange("p a t -> p (a t)").rearrange(
                "p (c q) -> p c q", c=NC)
            for qblk in range(2):
                for c in range(NC):
                    last = eng.matmul(
                        P1[:, qblk * 8 + c:qblk * 8 + c + 1],
                        pv[:, c, qblk * 128:(qblk + 1) * 128],
                        ones_col[:], start=True, stop=True)
            return last
        sch.add("pe", t_qn, reads=(r_p[0], r_p[1], r_c), writes=(r_sb[0],))

        def t_qcln(eng):
            return eng.activation(qcln[:], P1[:, 0:16], AF.Ln)
        sch.add("act", t_qcln, reads=(r_sb[0],), writes=(r_mh,))

    n_sync_dma = [0]

    def dma_sync(src_ap, dst_ap, reads, writes, chan):
        def t(eng, s=src_ap, d=dst_ap):
            return eng.dma_start(d, s)
        tid = sch.add("sync", t, reads=reads, writes=writes, is_dma=True,
                      chan=chan)
        n_sync_dma[0] += 1
        return tid

    x_re = x_d.rearrange("(c p) t -> p c t", p=128)
    out_re = out_d.rearrange("(c p) t -> p c t", p=128)

    for g in range(n_g):
        kv_units = []
        q_units = []
        v_defer = []
        if g == 0:
            dma_sync(x_re[:, :, 0:TSB], xr[:, 0],
                     reads=(), writes=(r_x[0],), chan="ld0")
        if g + 1 < NSB:
            nsl = (g + 1) % 3
            dma_sync(x_re[:, :, (g + 1) * TSB:(g + 2) * TSB], xr[:, nsl],
                     reads=(), writes=(r_x[nsl],), chan=f"ld{nsl}")
        if g < NSB:
            sl = g % 3

            def t_cast(eng, sl=sl):
                return eng.tensor_copy(xbv, xr[:, sl])
            sch.add("dve", t_cast, reads=(r_x[sl],), writes=(r_xbq,))
            emit_ln_stats(with_negm=False)

            def t_xhat(eng, sl=sl):
                return eng.tensor_tensor(xhat[:], xr[:, sl], rb_bc, ALU.mult)
            sch.add("dve", t_xhat, reads=(r_x[sl], r_s5), writes=(r_xhat,))

            qsl = g % 2
            s0 = (2 * g) % 6

            for p in range(4):
                def ke_fn(eng, p=p, bank=p % 2, s0=s0):
                    return eng.copy(
                        kr[:, 2 * p:2 * p + 2, s0:s0 + 2, :],
                        B45[bank][:, :].rearrange(
                            "p (c b q) -> p c b q", c=2, b=2))
                kv_units.append(proj_pair_unit(wk, ckrow, p, "act", ke_fn,
                                               (r_k[s0], r_k[s0 + 1])))
            for u in range(4):
                j, eh = u // 2, u % 2
                bank = u % 2 if u < 2 else 0

                def t_vmm(eng, j=j, eh=eh, bank=bank):
                    dst = B45[bank][:, :]
                    last = None
                    for kc in range(NC):
                        last = eng.matmul(
                            dst, xhat[:, kc, j * 128:(j + 1) * 128],
                            wv[:, kc, eh * 512:(eh + 1) * 512],
                            start=(kc == 0), stop=False)
                    last = eng.matmul(
                        dst, negmr[0:1, j * 128:(j + 1) * 128],
                        cvrow[0:1, eh * 512:(eh + 1) * 512],
                        start=False, stop=True)
                    return last

                def t_ve(eng, j=j, eh=eh, bank=bank, s0=s0):
                    return eng.activation(
                        vr[:, s0 + j, eh * 512:(eh + 1) * 512],
                        B45[bank][:, :], AF.Copy)

                def mk(vm=t_vmm, ve=t_ve, bank=bank, j=j, s0=s0):
                    def emit():
                        sch.add("pe", vm, reads=(r_w, r_xhat, r_ln),
                                writes=(RB45[bank],))
                        sch.add("act", ve, reads=(RB45[bank],),
                                writes=(r_v[s0 + j],))
                    return emit
                (kv_units if u < 2 else v_defer).append(mk())

            for p in range(4):
                def qe_fn(eng, p=p, bank=p % 2, qsl=qsl):
                    return eng.activation(
                        qr[:, qsl, 2 * p:2 * p + 2, :].rearrange(
                            "p c t -> p (c t)"),
                        B45[bank][:, :], AF.Copy, scale=0.125)
                q_units.append(proj_pair_unit(wq, cqrow, p, "act", qe_fn,
                                              (r_q[qsl],)))

        def pop_kv(n=1):
            for _ in range(n):
                if kv_units:
                    kv_units.pop(0)()

        def pop_q(n=1):
            for _ in range(n):
                if q_units:
                    q_units.pop(0)()

        if g >= 1:
            sb = g - 1
            sl = sb % 3
            qsl = sb % 2

            if STOP_AFTER == "preattn":
                continue
            wgs = []
            for qj in range(2):
                qb = 2 * sb + qj
                wins = [w for w in (qb - 1, qb) if 0 <= w <= 30]
                for wi, w in enumerate(wins):
                    wgs.append((qj, wi, w, len(wins)))

            def wg_front(si, qj, wi, w, nw, grp, qsl=qsl):
                pair = si % 2
                nmi = qj * 8 + wi * 4 + grp

                def t_smm(eng, qj=qj, grp=grp, w=w, pair=pair, qsl=qsl):
                    last = None
                    eng.matmul(PING[pair][:, 0:128], ident[:],
                               kr[:, grp * 2, w % 6, :], start=True, stop=True)
                    for parity in (0, 1):
                        if parity == 1:
                            eng.matmul(PING[pair][:, 256:384], ident[:],
                                       kr[:, grp * 2, w % 6, :],
                                       start=True, stop=True)
                        pb = 64 * parity
                        for hi in range(parity, 4, 2):
                            h = grp * 4 + hi
                            lhs = qr[pb:pb + 64, qsl, h // 2,
                                     qj * 128:(qj + 1) * 128]
                            if w % 6 <= 4:
                                dst = PING[pair][:, hi * 256:hi * 256 + 256]
                                last = eng.matmul(
                                    dst, lhs,
                                    kr[pb:pb + 64, h // 2,
                                       w % 6:w % 6 + 2, :].rearrange(
                                        "p b q -> p (b q)"),
                                    start=True, stop=True)
                                continue
                            for kbi, kb in enumerate((w, w + 1)):
                                dst = PING[pair][
                                    :, hi * 256 + kbi * 128:
                                    hi * 256 + kbi * 128 + 128]
                                last = eng.matmul(
                                    dst, lhs,
                                    kr[pb:pb + 64, h // 2, kb % 6, :],
                                    start=True, stop=True)
                    return last
                kres = (r_k[w % 6], r_k[(w + 1) % 6])
                sch.add("pe", t_smm, reads=kres + (r_q[qsl], r_c),
                        writes=(r_sb[2 * pair], r_sb[2 * pair + 1]))

                def t_exp(eng, pair=pair, nmi=nmi):
                    return eng.activation(pbuf[:, pair, :], PING[pair][:, :],
                                          AF.Exp,
                                          bias=negmh[:, nmi:nmi + 1])
                sch.add("act", t_exp,
                        reads=(r_sb[2 * pair], r_sb[2 * pair + 1],
                               r_nm[nmi], r_mh),
                        writes=(r_p[pair],))

                def t_z(eng, pair=pair, wi=wi, grp=grp):
                    return eng.tensor_reduce(
                        zbuf[:, wi, grp * 4:grp * 4 + 4].rearrange(
                            "p (h v) -> p h v", v=1),
                        pbuf[:, pair, :].rearrange("p (h t) -> p h t", h=4),
                        axis=AX.X, op=ALU.add)
                sch.add("dve", t_z, reads=(r_p[pair],), writes=(r_z,))

                def t_zinv(eng, wi=wi, grp=grp):
                    with nc.allow_low_precision(
                            reason="1/Z at bf16 matches bf16 P operand"):
                        return eng.reciprocal(
                            zinv[:, wi, grp * 4:grp * 4 + 4],
                            zbuf[:, wi, grp * 4:grp * 4 + 4])
                sch.add("dve", t_zinv, reads=(r_z,), writes=(r_zi,))

                def t_dbuild(eng, wi=wi, grp=grp, pair=pair):
                    iv = ident[:].rearrange(
                        "p (v q) -> p v q", v=1).broadcast_to([128, 4, 128])
                    zz = zinv[:, wi, grp * 4:grp * 4 + 4].rearrange(
                        "p (h v) -> p h v", v=1).broadcast_to([128, 4, 128])
                    return eng.tensor_tensor(dbuf[:, pair], iv, zz, ALU.mult)
                sch.add("dve", t_dbuild, reads=(r_zi, r_c),
                        writes=(r_d[pair],))

            def wg_tail(si, qj, wi, w, nw, grp):
                pair = si % 2

                def t_tr(eng, pair=pair):
                    last = None
                    for i in range(8):
                        hi = i // 2
                        last = eng.matmul(
                            PING[pair][:, i * 128:(i + 1) * 128],
                            pbuf[:, pair, i * 128:(i + 1) * 128],
                            dbuf[:, pair, hi, :], start=True, stop=True)
                    return last
                sch.add("pe", t_tr, reads=(r_p[pair], r_d[pair]),
                        writes=(r_sb[2 * pair], r_sb[2 * pair + 1]))

                def t_pte(eng, pair=pair, wi=wi):
                    return eng.copy(scr[:, wi * 1024:(wi + 1) * 1024],
                                    PING[pair][:, :])
                sch.add("act", t_pte,
                        reads=(r_sb[2 * pair], r_sb[2 * pair + 1]),
                        writes=(r_scr[2 * wi], r_scr[2 * wi + 1]))

                if wi != nw - 1:
                    return
                wins = [w2 for w2 in (2 * sb + qj - 1, 2 * sb + qj)
                        if 0 <= w2 <= 30]
                vres = tuple({r_v[w2 % 6] for w2 in wins} |
                             {r_v[(w2 + 1) % 6] for w2 in wins})

                def t_av(eng, grp=grp, wins=wins):
                    last = None
                    for hi in range(4):
                        h = grp * 4 + hi
                        pb = 64 * (h % 2)
                        nmm = 2 * len(wins)
                        i_mm = 0
                        for wi2, w2 in enumerate(wins):
                            for kbi, kb in enumerate((w2, w2 + 1)):
                                last = eng.matmul(
                                    CTXPS[pb:pb + 64,
                                          (h // 2) * 128:
                                          (h // 2) * 128 + 128],
                                    vr[:, kb % 6, h * 64:(h + 1) * 64],
                                    scr[:, wi2 * 1024 + hi * 256 + kbi * 128:
                                        wi2 * 1024 + hi * 256 +
                                        kbi * 128 + 128],
                                    start=(i_mm == 0), stop=(i_mm == nmm - 1),
                                    tile_position=(0, pb))
                                i_mm += 1
                    return last
                sch.add("pe", t_av, reads=vres + tuple(r_scr),
                        writes=(r_cx[grp // 2],))

                sc = 0.5 if len(wins) == 2 else 1.0

                def t_ctxe(eng, qj=qj, grp=grp, sc=sc):
                    return eng.activation(
                        ctx[:, 2 * grp:2 * grp + 2,
                            qj * 128:(qj + 1) * 128],
                        CTXPS[:, grp * 256:(grp + 1) * 256].rearrange(
                            "p (c q) -> p c q", c=2),
                        AF.Copy, scale=sc)
                sch.add("act", t_ctxe, reads=(r_cx[grp // 2], r_c),
                        writes=(r_ctx[grp // 2],))

            stream = [(qj, wi, w, nw, grp)
                      for qj, wi, w, nw in wgs for grp in range(4)]
            stream = []
            for qj, qwins in enumerate(QJW2 := [
                    [w for w in (2 * sb + qj2 - 1, 2 * sb + qj2)
                     if 0 <= w <= 30] for qj2 in range(2)]):
                for grp in range(4):
                    for wi, w in enumerate(qwins):
                        stream.append((qj, wi, w, len(qwins), grp))
            hook_si = next((si for si, wg in enumerate(stream)
                            if wg[0] == 1), None)
            if not (g < NSB and 2 * sb + 1 <= 30):
                hook_si = None
            pending = []
            stream = stream[:NWG]
            for si, wg in enumerate(stream):
                if si == hook_si and not NOHOOK:
                    while pending:
                        wg_tail(*pending.pop(0))
                    emit_ksq(g)
                    emit_negmh(sb, 1, 1)
                if len(pending) >= 2:
                    wg_tail(*pending.pop(0))
                wg_front(si, *wg)
                pending.append((si,) + wg)
                if not NOKV:
                    pop_kv(1)
            while pending:
                wg_tail(*pending.pop(0))
            pop_kv(2)

            if STOP_AFTER == "attn":
                continue
            for p in range(4):
                bank = p % 2

                def t_womm(eng, p=p, bank=bank):
                    last = None
                    for o in range(2):
                        oc = 2 * p + o
                        dst = B45[bank][:, o * TSB:(o + 1) * TSB]
                        for kc in range(NC):
                            last = eng.matmul(
                                dst, wo[:, kc, oc * 128:(oc + 1) * 128],
                                ctx[:, kc, :],
                                start=(kc == 0), stop=(kc == NC - 1))
                    return last
                sch.add("pe", t_womm, reads=(r_w,) + tuple(r_ctx),
                        writes=(RB45[bank],))

                def t_bre(eng, p=p, bank=bank):
                    return eng.tensor_copy(
                        xbv[:, 2 * p:2 * p + 2, :],
                        B45[bank][:, :].rearrange("p (c t) -> p c t", c=2))
                sch.add("dve", t_bre, reads=(RB45[bank],), writes=(r_xbq,))

            if STOP_AFTER == "wo":
                continue
            def t_tall(eng):
                rb = rbsb[:].rearrange("p (u t) -> p u t", u=1).broadcast_to(
                    [128, NC, TSB])
                return eng.tensor_tensor(
                    scr[:].rearrange("p (c t) -> p c t", c=NC),
                    xbv, rb, ALU.mult)

            def t_resid(eng, sl=sl):
                return eng.tensor_tensor(
                    xr[:, sl], xr[:, sl],
                    scr[:].rearrange("p (c t) -> p c t", c=NC), ALU.add)

            def t_residm(eng, sl=sl):
                return eng.tensor_tensor(xr[:, sl], xr[:, sl], negmr_bc,
                                         ALU.add)

            emit_ln_stats(with_negm=True)
            while v_defer:
                v_defer.pop(0)()
            sch.add("dve", t_tall, reads=(r_xbq, r_rb), writes=tuple(r_scr))
            sch.add("dve", t_resid, reads=(r_x[sl],) + tuple(r_scr),
                    writes=(r_x[sl],))
            sch.add("dve", t_residm, reads=(r_x[sl], r_s5),
                    writes=(r_x[sl],))

            if STOP_AFTER == "ln2":
                continue
            def t_cast2(eng, sl=sl):
                return eng.tensor_copy(xbv, xr[:, sl])
            sch.add("dve", t_cast2, reads=(r_x[sl],), writes=(r_xbq,))
            emit_ln_stats(with_negm=False, mean_dst=negmr3, evac_rb=True)

            if STOP_AFTER == "ln3":
                continue
            def u1e_fn(eng):
                rb = rbsb[:].rearrange("p (u t) -> p u t", u=1).broadcast_to(
                    [128, 2, TSB])
                return eng.tensor_tensor(
                    p2sb[:], B45[0][:, :].rearrange("p (c t) -> p c t", c=2),
                    rb, ALU.mult)
            proj_pair_unit(u1, cu1row, 0, "dve", u1e_fn, (r_p2sb,),
                           xsrc=xbv, negrow=negmr3,
                           rx=(r_xbq,), evac_reads=(r_rb,), rneg=r_mean)()

            u2_first = [None]

            def emit_v1(grp):
                ping = grp % 2

                def t_v1mm(eng, grp=grp, ping=ping):
                    last = None
                    for hc in range(4):
                        ffc = grp * 4 + hc
                        dst = PING[ping][:, hc * 256:(hc + 1) * 256]
                        for kc in range(2):
                            last = eng.matmul(
                                dst, v1[:, kc, ffc * 128:(ffc + 1) * 128],
                                p2sb[:, kc, :], start=(kc == 0),
                                stop=(kc == 1))
                    return last
                sch.add("pe", t_v1mm, reads=(r_w, r_p2sb),
                        writes=(r_sb[2 * ping], r_sb[2 * ping + 1]))

                def t_gelu(eng, grp=grp, ping=ping):
                    last = None
                    for hc in range(4):
                        ffc = grp * 4 + hc
                        last = eng.activation(
                            ctx[:, (grp % 2) * 4 + hc, :],
                            PING[ping][:, hc * 256:(hc + 1) * 256],
                            AF.Gelu_apprx_tanh,
                            bias=b1col[:, ffc:ffc + 1])
                    return last
                sch.add("act", t_gelu,
                        reads=(r_sb[2 * ping], r_sb[2 * ping + 1], r_w),
                        writes=(r_ctx[grp % 2],))

            def emit_u2(grp):
                def t_u2mm(eng, grp=grp):
                    last = None
                    for oc in range(2):
                        dst = (P2[:, 1024:1280] if oc == 0
                               else P2[:, 1536:1792])
                        for hc in range(4):
                            ffc = grp * 4 + hc
                            last = eng.matmul(
                                dst, u2[:, ffc, oc * 128:(oc + 1) * 128],
                                ctx[:, (grp % 2) * 4 + hc, :],
                                start=(grp == 0 and hc == 0),
                                stop=(grp == 7 and hc == 3))
                    return last
                if grp == 0:
                    u2_first[0] = sch.add("pe", t_u2mm,
                                          reads=(r_w, r_ctx[0]),
                                          writes=(r_cx[0], r_cx[1]))
                else:
                    sch.add("pe", t_u2mm, reads=(r_w, r_ctx[grp % 2]),
                            extra=(u2_first[0],))

            for grp in range(8):
                emit_v1(grp)
                if grp >= 1:
                    emit_u2(grp - 1)
                if grp % 2 == 1:
                    pop_q(1)
            emit_u2(7)
            r_cx[0].w = len(sch.tasks) - 1
            r_cx[1].w = len(sch.tasks) - 1
            if g < NSB:
                emit_qside_sq(g)

            def t_u2e(eng):
                eng.tensor_copy(q2sb[:, 0, :], P2[:, 1024:1280])
                return eng.tensor_copy(q2sb[:, 1, :], P2[:, 1536:1792])
            sch.add("dve", t_u2e, reads=(r_cx[0], r_cx[1]), writes=(r_q2sb,))

            for p in range(4):
                bank = p % 2

                def t_v2mm(eng, p=p, bank=bank):
                    last = None
                    for o in range(2):
                        oc = 2 * p + o
                        dst = B45[bank][:, o * TSB:(o + 1) * TSB]
                        for kc in range(2):
                            last = eng.matmul(
                                dst, v2[:, kc, oc * 128:(oc + 1) * 128],
                                q2sb[:, kc, :], start=(kc == 0),
                                stop=(kc == 1))
                    return last
                sch.add("pe", t_v2mm, reads=(r_w, r_q2sb),
                        writes=(RB45[bank],))

                def t_fbe(eng, p=p, bank=bank):
                    last = None
                    for o in range(2):
                        oc = 2 * p + o
                        last = eng.tensor_scalar(
                            xbv[:, oc, :],
                            B45[bank][:, o * TSB:(o + 1) * TSB],
                            b2col[:, oc:oc + 1], 0.0, ALU.add, ALU.add)
                    return last
                sch.add("dve", t_fbe, reads=(RB45[bank], r_w),
                        writes=(r_xbq,))

            if g < NSB:
                emit_qside(g)
                wins0 = [w for w in (2 * g - 1, 2 * g) if 0 <= w <= 30]
                for wi in range(len(wins0)):
                    emit_negmh(g, 0, wi)
                emit_negmh(g, 1, 0)

            emit_ln_stats(with_negm=True)
            sch.add("dve", t_tall, reads=(r_xbq, r_rb), writes=tuple(r_scr))
            sch.add("dve", t_resid, reads=(r_x[sl],) + tuple(r_scr),
                    writes=(r_x[sl],))
            sch.add("dve", t_residm, reads=(r_x[sl], r_s5),
                    writes=(r_x[sl],))

            dma_sync(xr[:, sl], out_re[:, :, sb * TSB:(sb + 1) * TSB],
                     reads=(r_x[sl],), writes=(), chan=f"st{sb % 3}")

        pop_kv(len(kv_units))
        while v_defer:
            v_defer.pop(0)()
        pop_q(len(q_units))

        if g == 0:
            emit_qside_sq(0)
            emit_qside(0)
            emit_ksq(0)
            wins0 = [w for w in (-1, 0) if 0 <= w <= 30]
            for wi in range(len(wins0)):
                emit_negmh(0, 0, wi)
            emit_negmh(0, 1, 0)

    return nc, sch


def emit_program(nc, sch):
    counts = {e: 0 for e in Sched.ENGINES}
    dma_counts = {}
    for t in sch.tasks:
        if t.is_dma:
            dma_counts[t.chan] = dma_counts.get(t.chan, 0) + 16
            t.sem_val = dma_counts[t.chan]
        else:
            counts[t.eng] += 1
            t.sem_val = counts[t.eng]

    with contextlib.ExitStack() as st:
        sems = {e: st.enter_context(nc.semaphore(f"sem_{e}"))
                for e in Sched.ENGINES}
        dsems = {ch: st.enter_context(nc.semaphore(f"dma_{ch}"))
                 for ch in dma_counts}
        block = st.enter_context(nc.Block())

        def emit_engine(eng_name, eng):
            best = {}
            for t in sch.tasks:
                if t.eng != eng_name:
                    continue
                for d in sorted(t.deps):
                    dt = sch.tasks[d]
                    if dt.is_dma:
                        key = ("dma", dt.chan)
                        sem = dsems[dt.chan]
                    else:
                        key = dt.eng
                        sem = sems[dt.eng]
                    v = dt.sem_val
                    if best.get(key, 0) >= v:
                        continue
                    eng.wait_ge(sem, v)
                    best[key] = v
                last = t.emit(eng)
                assert last is not None, t.tid
                if t.is_dma:
                    last.then_inc(dsems[t.chan], 16)
                else:
                    last.then_inc(sems[t.eng], 1)
            if eng_name == "sync":
                for ch in ("st0", "st1", "st2"):
                    if ch in dma_counts:
                        eng.wait_ge(dsems[ch], dma_counts[ch])

        @block.tensor
        def _(eng):
            emit_engine("pe", eng)

        @block.scalar
        def _(eng):
            emit_engine("act", eng)

        @block.vector
        def _(eng):
            emit_engine("dve", eng)

        @block.gpsimd
        def _(eng):
            emit_engine("pool", eng)

        @block.sync
        def _(eng):
            emit_engine("sync", eng)
    return nc


_CACHED = {}


def _get_nc():
    if "nc" not in _CACHED:
        nc, sch = build_nc()
        emit_program(nc, sch)
        _CACHED["nc"] = nc
    return _CACHED["nc"]


def kernel(L0_attn_hidden_states, L0_attn_Wq, L0_attn_Wk, L0_attn_Wv,
           L0_attn_Wo, L0_ffn_U1, L0_ffn_V1, L0_ffn_b1, L0_ffn_U2,
           L0_ffn_V2, L0_ffn_b2):
    x = np.asarray(L0_attn_hidden_states, dtype=np.float32)
    weights = {
        "wq": np.ascontiguousarray(np.asarray(L0_attn_Wq, np.float32)),
        "wk": np.ascontiguousarray(np.asarray(L0_attn_Wk, np.float32)),
        "wv": np.ascontiguousarray(np.asarray(L0_attn_Wv, np.float32)),
        "wo": np.ascontiguousarray(np.asarray(L0_attn_Wo, np.float32)),
        "u1": np.ascontiguousarray(np.asarray(L0_ffn_U1, np.float32)),
        "v1": np.ascontiguousarray(np.asarray(L0_ffn_V1, np.float32)),
        "b1": np.ascontiguousarray(np.asarray(L0_ffn_b1, np.float32)),
        "u2": np.ascontiguousarray(np.asarray(L0_ffn_U2, np.float32)),
        "v2": np.ascontiguousarray(np.asarray(L0_ffn_V2, np.float32)),
        "b2": np.ascontiguousarray(np.asarray(L0_ffn_b2, np.float32)),
    }
    nc = _get_nc()
    in_maps = []
    for b in range(8):
        m = dict(weights)
        m["x"] = np.ascontiguousarray(x[b].T)
        in_maps.append(m)
    res = run_bass_kernel_spmd(nc, in_maps, core_ids=list(range(8)))
    out = np.empty((8, S, D), np.float32)
    for b in range(8):
        out[b] = res.results[b]["out"].T
    return out



# version 18
# speedup vs baseline: 1.0804x; 1.0275x over previous
import contextlib

import numpy as np

import concourse.bass as bass
import concourse.mybir as mybir
from concourse.bass_utils import run_bass_kernel_spmd

F32 = mybir.dt.float32
BF16 = mybir.dt.bfloat16
AF = mybir.ActivationFunctionType
ALU = mybir.AluOpType
AX = mybir.AxisListType

S = 4096
D = 1024
NH = 16
DH = 64
TSB = 256
NSB = 16
DFF = 4096
RANK = 256
NC = 8

R_1024 = 1.0 / 1024.0
STOP_AFTER = None
GRPS = (0, 1, 2, 3)
NOHOOK = False
NOKV = False
NWG = 99


class Res:
    __slots__ = ("w", "rs", "name")

    def __init__(self, name=""):
        self.w = None
        self.rs = []
        self.name = name


class Task:
    __slots__ = ("tid", "eng", "emit", "deps", "is_dma", "chan", "sem_val")

    def __init__(self, tid, eng, emit, deps, is_dma, chan=None):
        self.tid = tid
        self.eng = eng
        self.emit = emit
        self.deps = deps
        self.is_dma = is_dma
        self.chan = chan
        self.sem_val = None


class Sched:
    ENGINES = ("pe", "act", "dve", "pool", "sync")

    def __init__(self):
        self.tasks = []

    def add(self, eng, emit, reads=(), writes=(), is_dma=False, extra=(),
            chan=None):
        deps = set(extra)
        for r in reads:
            if r.w is not None:
                deps.add(r.w)
        for r in writes:
            if r.w is not None:
                deps.add(r.w)
            deps.update(r.rs)
        t = Task(len(self.tasks), eng, emit, deps, is_dma, chan)
        self.tasks.append(t)
        for r in reads:
            r.rs.append(t.tid)
        for r in writes:
            r.w = t.tid
            r.rs = []
        return t.tid


def build_nc(n_g=NSB + 1):
    nc = bass.Bass("TRN2", target_bir_lowering=False)

    x_d = nc.dram_tensor("x", [D, S], F32, kind="ExternalInput").ap()
    out_d = nc.dram_tensor("out", [D, S], F32, kind="ExternalOutput").ap()
    wq_d = nc.dram_tensor("wq", [D, D], F32, kind="ExternalInput").ap()
    wk_d = nc.dram_tensor("wk", [D, D], F32, kind="ExternalInput").ap()
    wv_d = nc.dram_tensor("wv", [D, D], F32, kind="ExternalInput").ap()
    wo_d = nc.dram_tensor("wo", [D, D], F32, kind="ExternalInput").ap()
    u1_d = nc.dram_tensor("u1", [D, RANK], F32, kind="ExternalInput").ap()
    v1_d = nc.dram_tensor("v1", [RANK, DFF], F32, kind="ExternalInput").ap()
    b1_d = nc.dram_tensor("b1", [DFF], F32, kind="ExternalInput").ap()
    u2_d = nc.dram_tensor("u2", [DFF, RANK], F32, kind="ExternalInput").ap()
    v2_d = nc.dram_tensor("v2", [RANK, D], F32, kind="ExternalInput").ap()
    b2_d = nc.dram_tensor("b2", [D], F32, kind="ExternalInput").ap()

    A = nc.alloc_sbuf_tensor
    wq = A("wq_sb", [128, NC, D], BF16)
    wk = A("wk_sb", [128, NC, D], BF16)
    wv = A("wv_sb", [128, NC, D], BF16)
    wo = A("wo_sb", [128, NC, D], BF16)
    u1 = A("u1_sb", [128, NC, RANK], BF16)
    v1 = A("v1_sb", [128, 2, DFF], BF16)
    u2 = A("u2_sb", [128, 32, RANK], BF16)
    v2 = A("v2_sb", [128, 2, D], BF16)
    b1col = A("b1col", [128, 32], F32)
    b2col = A("b2col", [128, 8], F32)
    cqrow = A("cqrow", [1, D], BF16)
    ckrow = A("ckrow", [1, D], BF16)
    cvrow = A("cvrow", [1, D], BF16)
    cu1row = A("cu1row", [1, RANK], BF16)

    ones_col = A("ones_col", [128, 1], BF16)
    ones_f32 = A("ones_f32", [1, 128], F32)
    ident = A("ident", [128, 128], BF16)
    cst = A("cst", [128, 8], F32)
    negmh = A("negmh", [128, 16], F32)
    kcmaxr = A("kcmaxr", [1, 6, 8], F32)
    qcln = A("qcln", [128, 16], F32)
    mrow = A("mrow", [1, 64], F32)

    xr = A("x_ring", [128, 3, NC, TSB], F32)
    xbsq = A("xbsq", [128, NC, 512], BF16)
    xhat = A("xhat", [128, NC, TSB], BF16)
    negmr3 = A("negmr3_row", [1, TSB], BF16)
    scr = A("scr", [128, 2048], BF16)
    qr = A("q_ring", [128, 2, NC, TSB], BF16)
    kr = A("k_ring", [128, NC, 6, 128], BF16)
    vr = A("v_ring", [128, 6, D], BF16)
    pbuf = A("pbuf", [128, 2, 1024], BF16)
    x1hat = pbuf[:].rearrange("p a t -> p (a t)").rearrange(
        "p (c t) -> p c t", c=NC)
    dbuf = A("dbuf", [128, 2, 4, 128], BF16)
    zbuf = A("zbuf", [128, 2, NH], F32)
    zinv = A("zinv", [128, 2, NH], BF16)
    ctx = A("ctx_sb", [128, NC, TSB], BF16)
    p2sb = A("p2sb", [128, 2, TSB], BF16)
    q2sb = A("q2sb", [128, 2, TSB], BF16)
    rbsb = A("rb_sb", [128, TSB], F32)
    stats = A("stats_sb", [1, 512], F32)
    small = A("small_sb", [1, 768], F32)
    rstd = A("rstd_row", [1, TSB], F32)
    negmr = A("negmr_row", [1, TSB], BF16)
    negmrf = A("negmrf_row", [1, TSB], F32)

    P1 = nc.alloc_psum_tensor("P1", [128, 2048], F32)
    P2 = nc.alloc_psum_tensor("P2", [128, 2048], F32)

    CONSTS = {0: 0.125, 1: 0.5, 2: R_1024, 3: -R_1024, 4: 1e-5, 5: -0.5}
    for j, v in CONSTS.items():
        nc.const_aps.register(F32, float(v), cst[:, j:j + 1])

    sch = Sched()
    r_x = [Res(f"x{i}") for i in range(3)]
    r_xbq = Res("xbsq")
    r_xhat = Res("xhat")
    r_scr = [Res(f"scr{i}") for i in range(4)]
    r_q = [Res("q0"), Res("q1")]
    r_k = [Res(f"k{i}") for i in range(6)]
    r_v = [Res(f"v{i}") for i in range(6)]
    r_p = [Res(f"pbuf{i}") for i in range(4)]
    r_d = [Res("dbufA"), Res("dbufB")]
    r_z = Res("zbuf")
    r_zi = Res("zinv")
    r_nm = [Res(f"nm{i}") for i in range(16)]
    r_mh = Res("mhat")
    r_ctx = [Res("ctxA"), Res("ctxB")]
    r_p2sb = Res("p2sb")
    r_q2sb = Res("q2sb")
    r_rb = Res("rbsb")
    r_st = Res("stats")
    r_ln = Res("lnrows")
    r_mean = Res("meanrow")
    r_sb = [Res(f"P1b{i}") for i in range(4)]
    r_pA = r_sb[0]
    r_pB = r_sb[2]
    r_s4 = Res("P2bank4")
    r_s5 = Res("P2bank5")
    r_cx = [Res("P2ctx6"), Res("P2ctx7")]
    r_w = Res("weights")
    r_c = Res("consts")

    PING = [P1[:, 0:1024], P1[:, 1024:2048]]
    SBANK = [P1[:, i * 512:(i + 1) * 512] for i in range(4)]
    SCRS = [scr[:, i * 512:(i + 1) * 512] for i in range(4)]
    RPING = [r_pA, r_pB]
    B45 = [P2[:, 0:512], P2[:, 512:1024]]
    RB45 = [r_s4, r_s5]
    CTXPS = P2[:, 1024:2048]

    def t_init_memsets(eng):
        eng.memset(negmh[:], 0.0)
        eng.memset(kcmaxr[:], 1.0)
        eng.memset(qcln[:], 0.0)
        eng.memset(ones_col[:], 1.0)
        eng.memset(ones_f32[:], 1.0)
        last = None
        for j, v in CONSTS.items():
            last = eng.memset(cst[:, j:j + 1], float(v))
        return last
    sch.add("dve", t_init_memsets, writes=(r_c,))

    iota_scr = pbuf[:].rearrange("p a t -> p (a t)").bitcast(F32)

    def t_iota(eng):
        eng.iota(iota_scr[:, 0:128], pattern=[[0, 128]], channel_multiplier=1,
                 allow_small_or_imprecise_dtypes=True)
        return eng.iota(iota_scr[:, 128:256], pattern=[[1, 128]],
                        channel_multiplier=0,
                        allow_small_or_imprecise_dtypes=True)
    sch.add("pool", t_iota, writes=tuple(r_p))

    def t_ident(eng):
        return eng.tensor_tensor(ident[:], iota_scr[:, 0:128],
                                 iota_scr[:, 128:256], ALU.is_equal)
    sch.add("dve", t_ident, reads=tuple(r_p), writes=(r_c,))

    for wi_, (dst, src) in enumerate((
        (wq, wq_d.rearrange("(kc p) j -> p kc j", p=128)),
        (wk, wk_d.rearrange("(kc p) j -> p kc j", p=128)),
        (wv, wv_d.rearrange("(kc p) j -> p kc j", p=128)),
        (wo, wo_d.rearrange("(kc p) j -> p kc j", p=128)),
        (u1, u1_d.rearrange("(kc p) j -> p kc j", p=128)),
        (v1, v1_d.rearrange("(kc p) j -> p kc j", p=128)),
        (u2, u2_d.rearrange("(kc p) j -> p kc j", p=128)),
        (v2, v2_d.rearrange("(kc p) j -> p kc j", p=128)),
    )):
        sch.add("pool",
                (lambda d, s: lambda eng: eng.dma_start(d[:], s))(dst, src),
                writes=(r_w,), is_dma=True, chan=f"w{wi_}")

    b1stage = pbuf[:].rearrange("p a t -> p (a t)")[0:32, 0:128]
    b2stage = pbuf[:].rearrange("p a t -> p (a t)")[0:8, 128:256]
    sch.add("pool",
            lambda eng: eng.dma_start(b1stage,
                                      b1_d.rearrange("(c p) -> c p", p=128)),
            writes=tuple(r_p), is_dma=True, chan="w8")
    sch.add("pool",
            lambda eng: eng.dma_start(b2stage,
                                      b2_d.rearrange("(c p) -> c p", p=128)),
            writes=tuple(r_p), is_dma=True, chan="w9")

    def t_btr(eng):
        eng.matmul(P2[:, 0:32], b1stage, ident[0:32, 0:32],
                   start=True, stop=True)
        return eng.matmul(P2[:, 32:40], b2stage, ident[0:8, 0:8],
                          start=True, stop=True)
    sch.add("pe", t_btr, reads=tuple(r_p) + (r_c,), writes=(r_s4,))

    def t_bcol(eng):
        eng.tensor_copy(b1col[:], P2[:, 0:32])
        return eng.tensor_copy(b2col[:], P2[:, 32:40])
    sch.add("dve", t_bcol, reads=(r_s4,), writes=(r_w,))

    def t_crows_a(eng):
        last = None
        for wsb, base in ((wq, 0), (wk, 1024)):
            for half in range(2):
                for kc in range(NC):
                    last = eng.matmul(
                        P1[0:1, base + half * 512: base + (half + 1) * 512],
                        ones_col[:], wsb[:, kc, half * 512:(half + 1) * 512],
                        start=(kc == 0), stop=(kc == NC - 1))
        return last
    sch.add("pe", t_crows_a, reads=(r_w, r_c), writes=tuple(r_sb))

    def t_crows_a_evac(eng):
        eng.tensor_copy(cqrow[0:1, :], P1[0:1, 0:1024])
        return eng.tensor_copy(ckrow[0:1, :], P1[0:1, 1024:2048])
    sch.add("dve", t_crows_a_evac, reads=tuple(r_sb), writes=(r_w,))

    def t_crows_b(eng):
        last = None
        for half in range(2):
            for kc in range(NC):
                last = eng.matmul(P1[0:1, half * 512:(half + 1) * 512],
                                  ones_col[:],
                                  wv[:, kc, half * 512:(half + 1) * 512],
                                  start=(kc == 0), stop=(kc == NC - 1))
        for kc in range(NC):
            last = eng.matmul(P1[0:1, 1024:1280], ones_col[:], u1[:, kc, :],
                              start=(kc == 0), stop=(kc == NC - 1))
        return last
    sch.add("pe", t_crows_b, reads=(r_w, r_c) + tuple(r_sb), writes=tuple(r_sb))

    def t_crows_b_evac(eng):
        eng.tensor_copy(cvrow[0:1, :], P1[0:1, 0:1024])
        return eng.tensor_copy(cu1row[0:1, :], P1[0:1, 1024:1280])
    sch.add("dve", t_crows_b_evac, reads=tuple(r_sb), writes=(r_w,))

    rb_bc = P2[:, 512:768].rearrange("p (u t) -> p u t", u=1).broadcast_to(
        [128, NC, TSB])
    negmr_bc = P2[:, 768:1024].rearrange(
        "p (u t) -> p u t", u=1).broadcast_to([128, NC, TSB])
    xbv = xbsq[:].rearrange("p c (a t) -> p c a t", a=2)[:, :, 0, :]
    sqv_t = xbsq[:].rearrange("p c (a t) -> p c a t", a=2)[:, :, 1, :]

    def emit_ln_stats(with_negm, pre_negmr=None, negmr_dst=None,
                      mean_dst=None, evac_rb=False, skip_sq=False):
        if not skip_sq:
            def t_sq(eng):
                return eng.tensor_tensor(sqv_t, xbv, xbv, ALU.mult)
            sch.add("dve", t_sq, reads=(r_xbq,), writes=(r_xbq,))

        def t_sums(eng):
            last = None
            for c in range(NC):
                last = eng.matmul(P2[0:1, 0:512], ones_col[:], xbsq[:, c, :],
                                  start=(c == 0), stop=(c == NC - 1))
            return last
        sch.add("pe", t_sums, reads=(r_xbq, r_c), writes=(r_s4,))

        def t_stats_evac(eng):
            return eng.tensor_copy(stats[0:1, :], P2[0:1, 0:512])
        sch.add("dve", t_stats_evac, reads=(r_s4,), writes=(r_st,))

        sumv = stats[0:1, 0:TSB]
        sqv = stats[0:1, TSB:512]

        if mean_dst is not None:
            def t_mean(eng):
                return eng.tensor_single_scalar(mean_dst[0:1, :], sumv,
                                                -R_1024, ALU.mult)
            sch.add("dve", t_mean, reads=(r_st, r_c), writes=(r_mean,))

        def t_small_dve1(eng):
            return eng.scalar_tensor_tensor(small[0:1, 0:TSB], sumv, R_1024,
                                            sumv, ALU.mult, ALU.mult)
        sch.add("dve", t_small_dve1, reads=(r_st, r_c), writes=(r_st,))

        def t_small_dve2(eng):
            return eng.tensor_tensor(small[0:1, TSB:512], sqv,
                                     small[0:1, 0:TSB], ALU.subtract)
        sch.add("dve", t_small_dve2, reads=(r_st,), writes=(r_st,))

        def t_small_act1(eng):
            return eng.activation(small[0:1, 512:768], small[0:1, TSB:512],
                                  AF.Ln, scale=R_1024, bias=1e-5)
        sch.add("act", t_small_act1, reads=(r_st, r_c), writes=(r_st,))

        def t_small_act2(eng):
            return eng.activation(rstd[0:1, :], small[0:1, 512:768], AF.Exp,
                                  scale=-0.5)
        sch.add("act", t_small_act2, reads=(r_st, r_c), writes=(r_st, r_ln))

        if pre_negmr is not None:
            pre_negmr()

        if mean_dst is None:
            def t_negmr(eng):
                dst = negmrf[0:1, :] if with_negm else (
                    negmr_dst if negmr_dst is not None else negmr)[0:1, :]
                return eng.scalar_tensor_tensor(dst, sumv, -R_1024,
                                                rstd[0:1, :], ALU.mult,
                                                ALU.mult)
            sch.add("dve", t_negmr, reads=(r_st, r_ln, r_c), writes=(r_ln,))

        def t_bcast(eng):
            last = eng.matmul(P2[:, 512:768], ones_f32[:], rstd[0:1, :],
                              start=True, stop=True)
            if with_negm:
                last = eng.matmul(P2[:, 768:1024], ones_f32[:], negmrf[0:1, :],
                                  start=True, stop=True)
            return last
        sch.add("pe", t_bcast, reads=(r_ln, r_c), writes=(r_s5,))

        if with_negm or evac_rb:
            def t_rb_evac(eng):
                return eng.tensor_copy(rbsb[:], P2[:, 512:768])
            sch.add("dve", t_rb_evac, reads=(r_s5,), writes=(r_rb,))

    def proj_pair_unit(wsb, crow, p, evac_eng, evac_fn, evac_writes,
                       xsrc=None, negrow=None, rx=None, evac_reads=(),
                       rneg=None):
        bank = p % 2
        xsrc = xhat if xsrc is None else xsrc
        negrow = negmr if negrow is None else negrow
        rx = (r_xhat,) if rx is None else rx
        rneg = r_ln if rneg is None else rneg

        def t_mm(eng, p=p, bank=bank, wsb=wsb, crow=crow):
            last = None
            for o in range(2):
                oc = 2 * p + o
                dst = B45[bank][:, o * TSB:(o + 1) * TSB]
                for kc in range(NC):
                    last = eng.matmul(dst,
                                      wsb[:, kc, oc * 128:(oc + 1) * 128],
                                      xsrc[:, kc, :],
                                      start=(kc == 0), stop=False)
                last = eng.matmul(dst, crow[0:1, oc * 128:(oc + 1) * 128],
                                  negrow[0:1, :], start=False, stop=True)
            return last

        def emit():
            sch.add("pe", t_mm, reads=(r_w, rneg) + tuple(rx),
                    writes=(RB45[bank],))
            sch.add(evac_eng, evac_fn, reads=(RB45[bank],) + tuple(evac_reads),
                    writes=tuple(evac_writes))
        return emit

    r_kc = [Res(f"kc{i}") for i in range(6)]
    mh_scr = nc.alloc_sbuf_tensor("mh_scr", [128, 8], F32)

    def emit_ksq(g):
        s0 = (2 * g) % 6

        def t_ksq(eng, s0=s0):
            kv = kr[:, :, s0:s0 + 2, :]
            sv = scr[:].rearrange("p (c b q) -> p c b q", c=NC, b=2)
            return eng.tensor_tensor(sv, kv, kv, ALU.mult)
        sch.add("pool", t_ksq, reads=(r_k[s0], r_k[s0 + 1]),
                writes=tuple(r_scr))

        for b in range(2):
            def t_kcmm(eng, b=b):
                last = None
                sv = scr[:].rearrange("p (c b q) -> p c b q", c=NC, b=2)
                for c in range(NC):
                    last = eng.matmul(P2[0:1, c * 128:(c + 1) * 128],
                                      ones_col[:], sv[:, c, b, :],
                                      start=True, stop=True)
                return last
            sch.add("pe", t_kcmm, reads=tuple(r_scr) + (r_c,),
                    writes=(r_s4, r_s5))

            def t_kcred(eng, b=b, s0=s0):
                return eng.tensor_reduce(
                    kcmaxr[0:1, s0 + b, :].rearrange("a (c v) -> a c v", v=1),
                    P2[0:1, 0:1024].rearrange("a (c q) -> a c q", c=NC),
                    axis=AX.X, op=ALU.max)
            sch.add("dve", t_kcred, reads=(r_s4, r_s5),
                    writes=(r_kc[s0 + b],))

    def emit_negmh(sbx, qj, wi):
        qbx = 2 * sbx + qj
        wlist = [w for w in (qbx - 1, qbx) if 0 <= w <= 30]
        w = wlist[wi]
        idx = qj * 2 + wi

        def t_kw(eng, w=w, idx=idx):
            return eng.tensor_tensor(mrow[0:1, idx * 8:(idx + 1) * 8],
                                     kcmaxr[0:1, w % 6, :],
                                     kcmaxr[0:1, (w + 1) % 6, :], ALU.max)
        sch.add("dve", t_kw, reads=(r_kc[w % 6], r_kc[(w + 1) % 6]),
                writes=(r_mh,))

        def t_lnkw(eng, idx=idx):
            return eng.activation(mrow[0:1, 32 + idx * 8:40 + idx * 8],
                                  mrow[0:1, idx * 8:(idx + 1) * 8], AF.Ln)
        sch.add("act", t_lnkw, reads=(r_mh,), writes=(r_mh,))

        def t_bc(eng, idx=idx):
            return eng.matmul(P2[:, 1016:1024], ones_f32[:],
                              mrow[0:1, 32 + idx * 8:40 + idx * 8],
                              start=True, stop=True)
        sch.add("pe", t_bc, reads=(r_mh, r_c), writes=(r_s5,))

        def t_add(eng, qj=qj):
            return eng.tensor_tensor(mh_scr[:], qcln[:, qj * 8:qj * 8 + 8],
                                     P2[:, 1016:1024], ALU.add)
        sch.add("dve", t_add, reads=(r_mh, r_s5), writes=(r_mh,))

        def t_exp2(eng):
            return eng.activation(mh_scr[:], mh_scr[:], AF.Exp, scale=0.5)
        sch.add("act", t_exp2, reads=(r_mh, r_c), writes=(r_mh,))

        def t_red(eng, qj=qj, wi=wi):
            return eng.reduce_max(
                negmh[:, qj * 8 + wi * 4:qj * 8 + wi * 4 + 4].rearrange(
                    "p (g v) -> p g v", v=1),
                mh_scr[:].rearrange("p (g c) -> p g c", g=4),
                axis=AX.X, negate=True)
        nres = tuple(r_nm[qj * 8 + wi * 4 + grp] for grp in range(4))
        sch.add("dve", t_red, reads=(r_mh,), writes=nres + (r_mh,))

    def emit_qside_sq(g):
        qsl_ = g % 2

        def t_qsq(eng, qsl_=qsl_):
            pv = pbuf[:].rearrange("p a t -> p (a t)").rearrange(
                "p (c q) -> p c q", c=NC)
            return eng.tensor_tensor(pv, qr[:, qsl_], qr[:, qsl_], ALU.mult)
        sch.add("pool", t_qsq, reads=(r_q[qsl_],), writes=(r_p[0], r_p[1]))

    def emit_qside(g):
        def t_qn(eng):
            last = None
            pv = pbuf[:].rearrange("p a t -> p (a t)").rearrange(
                "p (c q) -> p c q", c=NC)
            for qblk in range(2):
                for c in range(NC):
                    last = eng.matmul(
                        P1[:, qblk * 8 + c:qblk * 8 + c + 1],
                        pv[:, c, qblk * 128:(qblk + 1) * 128],
                        ones_col[:], start=True, stop=True)
            return last
        sch.add("pe", t_qn, reads=(r_p[0], r_p[1], r_c), writes=(r_sb[0],))

        def t_qcln(eng):
            return eng.activation(qcln[:], P1[:, 0:16], AF.Ln)
        sch.add("act", t_qcln, reads=(r_sb[0],), writes=(r_mh,))

    n_sync_dma = [0]

    def dma_sync(src_ap, dst_ap, reads, writes, chan):
        def t(eng, s=src_ap, d=dst_ap):
            return eng.dma_start(d, s)
        tid = sch.add("sync", t, reads=reads, writes=writes, is_dma=True,
                      chan=chan)
        n_sync_dma[0] += 1
        return tid

    x_re = x_d.rearrange("(c p) t -> p c t", p=128)
    out_re = out_d.rearrange("(c p) t -> p c t", p=128)

    for g in range(n_g):
        kv_units = []
        q_units = []
        v_defer = []
        if g == 0:
            dma_sync(x_re[:, :, 0:TSB], xr[:, 0],
                     reads=(), writes=(r_x[0],), chan="ld0")
        if g + 1 < NSB:
            nsl = (g + 1) % 3
            dma_sync(x_re[:, :, (g + 1) * TSB:(g + 2) * TSB], xr[:, nsl],
                     reads=(), writes=(r_x[nsl],), chan=f"ld{nsl}")
        if g < NSB:
            sl = g % 3

            if g == 0:
                def t_cast(eng, sl=sl):
                    return eng.tensor_copy(xbv, xr[:, sl])
                sch.add("dve", t_cast, reads=(r_x[sl],), writes=(r_xbq,))
                emit_ln_stats(with_negm=False)

                def t_xhat(eng, sl=sl):
                    return eng.tensor_tensor(xhat[:], xr[:, sl], rb_bc,
                                             ALU.mult)
                sch.add("dve", t_xhat, reads=(r_x[sl], r_s5),
                        writes=(r_xhat,))

            qsl = g % 2
            s0 = (2 * g) % 6

            for p in range(4):
                def ke_fn(eng, p=p, bank=p % 2, s0=s0):
                    return eng.copy(
                        kr[:, 2 * p:2 * p + 2, s0:s0 + 2, :],
                        B45[bank][:, :].rearrange(
                            "p (c b q) -> p c b q", c=2, b=2))
                kv_units.append(proj_pair_unit(wk, ckrow, p, "act", ke_fn,
                                               (r_k[s0], r_k[s0 + 1])))
            for u in range(4):
                j, eh = u // 2, u % 2
                bank = u % 2 if u < 2 else 0

                def t_vmm(eng, j=j, eh=eh, bank=bank):
                    dst = B45[bank][:, :]
                    last = None
                    for kc in range(NC):
                        last = eng.matmul(
                            dst, xhat[:, kc, j * 128:(j + 1) * 128],
                            wv[:, kc, eh * 512:(eh + 1) * 512],
                            start=(kc == 0), stop=False)
                    last = eng.matmul(
                        dst, negmr[0:1, j * 128:(j + 1) * 128],
                        cvrow[0:1, eh * 512:(eh + 1) * 512],
                        start=False, stop=True)
                    return last

                def t_ve(eng, j=j, eh=eh, bank=bank, s0=s0):
                    return eng.activation(
                        vr[:, s0 + j, eh * 512:(eh + 1) * 512],
                        B45[bank][:, :], AF.Copy)

                def mk(vm=t_vmm, ve=t_ve, bank=bank, j=j, s0=s0):
                    def emit():
                        sch.add("pe", vm, reads=(r_w, r_xhat, r_ln),
                                writes=(RB45[bank],))
                        sch.add("act", ve, reads=(RB45[bank],),
                                writes=(r_v[s0 + j],))
                    return emit
                (kv_units if u < 2 else v_defer).append(mk())

            for p in range(4):
                def qe_fn(eng, p=p, bank=p % 2, qsl=qsl):
                    return eng.activation(
                        qr[:, qsl, 2 * p:2 * p + 2, :].rearrange(
                            "p c t -> p (c t)"),
                        B45[bank][:, :], AF.Copy, scale=0.125)
                q_units.append(proj_pair_unit(wq, cqrow, p, "act", qe_fn,
                                              (r_q[qsl],)))

        def pop_kv(n=1):
            for _ in range(n):
                if kv_units:
                    kv_units.pop(0)()

        def pop_q(n=1):
            for _ in range(n):
                if q_units:
                    q_units.pop(0)()

        if g >= 1:
            sb = g - 1
            sl = sb % 3
            qsl = sb % 2

            if STOP_AFTER == "preattn":
                continue
            wgs = []
            for qj in range(2):
                qb = 2 * sb + qj
                wins = [w for w in (qb - 1, qb) if 0 <= w <= 30]
                for wi, w in enumerate(wins):
                    wgs.append((qj, wi, w, len(wins)))

            def wg_front(si, qj, wi, w, nw, grp, qsl=qsl):
                pair = si % 2
                nmi = qj * 8 + wi * 4 + grp

                def t_smm(eng, qj=qj, grp=grp, w=w, pair=pair, qsl=qsl):
                    last = None
                    eng.matmul(PING[pair][:, 0:128], ident[:],
                               kr[:, grp * 2, w % 6, :], start=True, stop=True)
                    for parity in (0, 1):
                        if parity == 1:
                            eng.matmul(PING[pair][:, 256:384], ident[:],
                                       kr[:, grp * 2, w % 6, :],
                                       start=True, stop=True)
                        pb = 64 * parity
                        for hi in range(parity, 4, 2):
                            h = grp * 4 + hi
                            lhs = qr[pb:pb + 64, qsl, h // 2,
                                     qj * 128:(qj + 1) * 128]
                            if w % 6 <= 4:
                                dst = PING[pair][:, hi * 256:hi * 256 + 256]
                                last = eng.matmul(
                                    dst, lhs,
                                    kr[pb:pb + 64, h // 2,
                                       w % 6:w % 6 + 2, :].rearrange(
                                        "p b q -> p (b q)"),
                                    start=True, stop=True)
                                continue
                            for kbi, kb in enumerate((w, w + 1)):
                                dst = PING[pair][
                                    :, hi * 256 + kbi * 128:
                                    hi * 256 + kbi * 128 + 128]
                                last = eng.matmul(
                                    dst, lhs,
                                    kr[pb:pb + 64, h // 2, kb % 6, :],
                                    start=True, stop=True)
                    return last
                kres = (r_k[w % 6], r_k[(w + 1) % 6])
                sch.add("pe", t_smm, reads=kres + (r_q[qsl], r_c),
                        writes=(r_sb[2 * pair], r_sb[2 * pair + 1]))

                def t_exp(eng, pair=pair, nmi=nmi):
                    return eng.activation(pbuf[:, pair, :], PING[pair][:, :],
                                          AF.Exp,
                                          bias=negmh[:, nmi:nmi + 1])
                sch.add("act", t_exp,
                        reads=(r_sb[2 * pair], r_sb[2 * pair + 1],
                               r_nm[nmi], r_mh),
                        writes=(r_p[pair],))

                def t_z(eng, pair=pair, wi=wi, grp=grp):
                    return eng.tensor_reduce(
                        zbuf[:, wi, grp * 4:grp * 4 + 4].rearrange(
                            "p (h v) -> p h v", v=1),
                        pbuf[:, pair, :].rearrange("p (h t) -> p h t", h=4),
                        axis=AX.X, op=ALU.add)
                sch.add("dve", t_z, reads=(r_p[pair],), writes=(r_z,))

                def t_zinv(eng, wi=wi, grp=grp):
                    with nc.allow_low_precision(
                            reason="1/Z at bf16 matches bf16 P operand"):
                        return eng.reciprocal(
                            zinv[:, wi, grp * 4:grp * 4 + 4],
                            zbuf[:, wi, grp * 4:grp * 4 + 4])
                sch.add("dve", t_zinv, reads=(r_z,), writes=(r_zi,))

                def t_dbuild(eng, wi=wi, grp=grp, pair=pair):
                    iv = ident[:].rearrange(
                        "p (v q) -> p v q", v=1).broadcast_to([128, 4, 128])
                    zz = zinv[:, wi, grp * 4:grp * 4 + 4].rearrange(
                        "p (h v) -> p h v", v=1).broadcast_to([128, 4, 128])
                    return eng.tensor_tensor(dbuf[:, pair], iv, zz, ALU.mult)
                sch.add("dve", t_dbuild, reads=(r_zi, r_c),
                        writes=(r_d[pair],))

            def wg_tail(si, qj, wi, w, nw, grp):
                pair = si % 2

                def t_tr(eng, pair=pair):
                    last = None
                    for i in range(8):
                        hi = i // 2
                        last = eng.matmul(
                            PING[pair][:, i * 128:(i + 1) * 128],
                            pbuf[:, pair, i * 128:(i + 1) * 128],
                            dbuf[:, pair, hi, :], start=True, stop=True)
                    return last
                sch.add("pe", t_tr, reads=(r_p[pair], r_d[pair]),
                        writes=(r_sb[2 * pair], r_sb[2 * pair + 1]))

                def t_pte(eng, pair=pair, wi=wi):
                    return eng.copy(scr[:, wi * 1024:(wi + 1) * 1024],
                                    PING[pair][:, :])
                sch.add("act", t_pte,
                        reads=(r_sb[2 * pair], r_sb[2 * pair + 1]),
                        writes=(r_scr[2 * wi], r_scr[2 * wi + 1]))

                if wi != nw - 1:
                    return
                wins = [w2 for w2 in (2 * sb + qj - 1, 2 * sb + qj)
                        if 0 <= w2 <= 30]
                vres = tuple({r_v[w2 % 6] for w2 in wins} |
                             {r_v[(w2 + 1) % 6] for w2 in wins})

                def t_av(eng, grp=grp, wins=wins):
                    last = None
                    for hi in range(4):
                        h = grp * 4 + hi
                        pb = 64 * (h % 2)
                        nmm = 2 * len(wins)
                        i_mm = 0
                        for wi2, w2 in enumerate(wins):
                            for kbi, kb in enumerate((w2, w2 + 1)):
                                last = eng.matmul(
                                    CTXPS[pb:pb + 64,
                                          (h // 2) * 128:
                                          (h // 2) * 128 + 128],
                                    vr[:, kb % 6, h * 64:(h + 1) * 64],
                                    scr[:, wi2 * 1024 + hi * 256 + kbi * 128:
                                        wi2 * 1024 + hi * 256 +
                                        kbi * 128 + 128],
                                    start=(i_mm == 0), stop=(i_mm == nmm - 1),
                                    tile_position=(0, pb))
                                i_mm += 1
                    return last
                sch.add("pe", t_av, reads=vres + tuple(r_scr),
                        writes=(r_cx[grp // 2],))

                sc = 0.5 if len(wins) == 2 else 1.0

                def t_ctxe(eng, qj=qj, grp=grp, sc=sc):
                    return eng.activation(
                        ctx[:, 2 * grp:2 * grp + 2,
                            qj * 128:(qj + 1) * 128],
                        CTXPS[:, grp * 256:(grp + 1) * 256].rearrange(
                            "p (c q) -> p c q", c=2),
                        AF.Copy, scale=sc)
                sch.add("act", t_ctxe, reads=(r_cx[grp // 2], r_c),
                        writes=(r_ctx[grp // 2],))

            stream = [(qj, wi, w, nw, grp)
                      for qj, wi, w, nw in wgs for grp in range(4)]
            stream = []
            for qj, qwins in enumerate(QJW2 := [
                    [w for w in (2 * sb + qj2 - 1, 2 * sb + qj2)
                     if 0 <= w <= 30] for qj2 in range(2)]):
                for grp in range(4):
                    for wi, w in enumerate(qwins):
                        stream.append((qj, wi, w, len(qwins), grp))
            hook_si = next((si for si, wg in enumerate(stream)
                            if wg[0] == 1), None)
            if not (g < NSB and 2 * sb + 1 <= 30):
                hook_si = None
            pending = []
            stream = stream[:NWG]
            for si, wg in enumerate(stream):
                if si == hook_si and not NOHOOK:
                    while pending:
                        wg_tail(*pending.pop(0))
                    emit_ksq(g)
                    emit_negmh(sb, 1, 1)
                if len(pending) >= 2:
                    wg_tail(*pending.pop(0))
                wg_front(si, *wg)
                pending.append((si,) + wg)
                if not NOKV:
                    pop_kv(1)
            while pending:
                wg_tail(*pending.pop(0))
            pop_kv(2)

            if STOP_AFTER == "attn":
                continue
            for p in range(4):
                bank = p % 2

                def t_womm(eng, p=p, bank=bank):
                    last = None
                    for o in range(2):
                        oc = 2 * p + o
                        dst = B45[bank][:, o * TSB:(o + 1) * TSB]
                        for kc in range(NC):
                            last = eng.matmul(
                                dst, wo[:, kc, oc * 128:(oc + 1) * 128],
                                ctx[:, kc, :],
                                start=(kc == 0), stop=(kc == NC - 1))
                    return last
                sch.add("pe", t_womm, reads=(r_w,) + tuple(r_ctx),
                        writes=(RB45[bank],))

                def t_bre(eng, p=p, bank=bank):
                    return eng.tensor_copy(
                        xbv[:, 2 * p:2 * p + 2, :],
                        B45[bank][:, :].rearrange("p (c t) -> p c t", c=2))
                sch.add("dve", t_bre, reads=(RB45[bank],), writes=(r_xbq,))

            if STOP_AFTER == "wo":
                continue
            def t_tall(eng):
                rb = rbsb[:].rearrange("p (u t) -> p u t", u=1).broadcast_to(
                    [128, NC, TSB])
                return eng.tensor_tensor(
                    scr[:].rearrange("p (c t) -> p c t", c=NC),
                    xbv, rb, ALU.mult)

            def t_resid(eng, sl=sl):
                return eng.tensor_tensor(
                    xr[:, sl], xr[:, sl],
                    scr[:].rearrange("p (c t) -> p c t", c=NC), ALU.add)

            def t_residm(eng, sl=sl):
                return eng.tensor_tensor(xr[:, sl], xr[:, sl], negmr_bc,
                                         ALU.add)

            emit_ln_stats(with_negm=True)
            while v_defer:
                v_defer.pop(0)()
            sch.add("dve", t_tall, reads=(r_xbq, r_rb), writes=tuple(r_scr))
            sch.add("dve", t_resid, reads=(r_x[sl],) + tuple(r_scr),
                    writes=(r_x[sl],))
            sch.add("dve", t_residm, reads=(r_x[sl], r_s5),
                    writes=(r_x[sl],))

            if STOP_AFTER == "ln2":
                continue
            def t_cast2(eng, sl=sl):
                return eng.tensor_copy(xbv, xr[:, sl])
            sch.add("dve", t_cast2, reads=(r_x[sl],), writes=(r_xbq,))
            emit_ln_stats(with_negm=False, mean_dst=negmr3, evac_rb=True)

            if STOP_AFTER == "ln3":
                continue
            def u1e_fn(eng):
                rb = rbsb[:].rearrange("p (u t) -> p u t", u=1).broadcast_to(
                    [128, 2, TSB])
                return eng.tensor_tensor(
                    p2sb[:], B45[0][:, :].rearrange("p (c t) -> p c t", c=2),
                    rb, ALU.mult)
            proj_pair_unit(u1, cu1row, 0, "dve", u1e_fn, (r_p2sb,),
                           xsrc=xbv, negrow=negmr3,
                           rx=(r_xbq,), evac_reads=(r_rb,), rneg=r_mean)()

            u2_first = [None]

            def emit_v1(grp):
                ping = grp % 2

                def t_v1mm(eng, grp=grp, ping=ping):
                    last = None
                    for hc in range(4):
                        ffc = grp * 4 + hc
                        dst = PING[ping][:, hc * 256:(hc + 1) * 256]
                        for kc in range(2):
                            last = eng.matmul(
                                dst, v1[:, kc, ffc * 128:(ffc + 1) * 128],
                                p2sb[:, kc, :], start=(kc == 0),
                                stop=(kc == 1))
                    return last
                sch.add("pe", t_v1mm, reads=(r_w, r_p2sb),
                        writes=(r_sb[2 * ping], r_sb[2 * ping + 1]))

                def t_gelu(eng, grp=grp, ping=ping):
                    last = None
                    for hc in range(4):
                        ffc = grp * 4 + hc
                        last = eng.activation(
                            ctx[:, (grp % 2) * 4 + hc, :],
                            PING[ping][:, hc * 256:(hc + 1) * 256],
                            AF.Gelu_apprx_tanh,
                            bias=b1col[:, ffc:ffc + 1])
                    return last
                sch.add("act", t_gelu,
                        reads=(r_sb[2 * ping], r_sb[2 * ping + 1], r_w),
                        writes=(r_ctx[grp % 2],))

            def emit_u2(grp):
                def t_u2mm(eng, grp=grp):
                    last = None
                    for oc in range(2):
                        dst = (P2[:, 1024:1280] if oc == 0
                               else P2[:, 1536:1792])
                        for hc in range(4):
                            ffc = grp * 4 + hc
                            last = eng.matmul(
                                dst, u2[:, ffc, oc * 128:(oc + 1) * 128],
                                ctx[:, (grp % 2) * 4 + hc, :],
                                start=(grp == 0 and hc == 0),
                                stop=(grp == 7 and hc == 3))
                    return last
                if grp == 0:
                    u2_first[0] = sch.add("pe", t_u2mm,
                                          reads=(r_w, r_ctx[0]),
                                          writes=(r_cx[0], r_cx[1]))
                else:
                    sch.add("pe", t_u2mm, reads=(r_w, r_ctx[grp % 2]),
                            extra=(u2_first[0],))

            for grp in range(8):
                emit_v1(grp)
                if grp >= 1:
                    emit_u2(grp - 1)
                if grp % 2 == 1:
                    pop_q(1)
            emit_u2(7)
            r_cx[0].w = len(sch.tasks) - 1
            r_cx[1].w = len(sch.tasks) - 1
            if g < NSB:
                emit_qside_sq(g)

            def t_u2e(eng):
                eng.tensor_copy(q2sb[:, 0, :], P2[:, 1024:1280])
                return eng.tensor_copy(q2sb[:, 1, :], P2[:, 1536:1792])
            sch.add("dve", t_u2e, reads=(r_cx[0], r_cx[1]), writes=(r_q2sb,))

            for p in range(4):
                bank = p % 2

                def t_v2mm(eng, p=p, bank=bank):
                    last = None
                    for o in range(2):
                        oc = 2 * p + o
                        dst = B45[bank][:, o * TSB:(o + 1) * TSB]
                        for kc in range(2):
                            last = eng.matmul(
                                dst, v2[:, kc, oc * 128:(oc + 1) * 128],
                                q2sb[:, kc, :], start=(kc == 0),
                                stop=(kc == 1))
                    return last
                sch.add("pe", t_v2mm, reads=(r_w, r_q2sb),
                        writes=(RB45[bank],))

                def t_fbe(eng, p=p, bank=bank):
                    last = None
                    for o in range(2):
                        oc = 2 * p + o
                        last = eng.tensor_scalar(
                            xbv[:, oc, :],
                            B45[bank][:, o * TSB:(o + 1) * TSB],
                            b2col[:, oc:oc + 1], 0.0, ALU.add, ALU.add)
                    return last
                sch.add("dve", t_fbe, reads=(RB45[bank], r_w),
                        writes=(r_xbq,))

            if g < NSB:
                emit_qside(g)
                wins0 = [w for w in (2 * g - 1, 2 * g) if 0 <= w <= 30]
                for wi in range(len(wins0)):
                    emit_negmh(g, 0, wi)
                emit_negmh(g, 1, 0)

            emit_ln_stats(with_negm=True)
            sch.add("dve", t_tall, reads=(r_xbq, r_rb), writes=tuple(r_scr))
            if g + 1 < NSB:
                nsl2 = (g + 1) % 3

                def t_cast_n(eng, nsl2=nsl2):
                    return eng.tensor_copy(xbv, xr[:, nsl2])
                sch.add("dve", t_cast_n, reads=(r_x[nsl2],), writes=(r_xbq,))

                def t_sq_n(eng):
                    return eng.tensor_tensor(sqv_t, xbv, xbv, ALU.mult)
                sch.add("dve", t_sq_n, reads=(r_xbq,), writes=(r_xbq,))
            sch.add("dve", t_resid, reads=(r_x[sl],) + tuple(r_scr),
                    writes=(r_x[sl],))
            sch.add("dve", t_residm, reads=(r_x[sl], r_s5),
                    writes=(r_x[sl],))

            dma_sync(xr[:, sl], out_re[:, :, sb * TSB:(sb + 1) * TSB],
                     reads=(r_x[sl],), writes=(), chan=f"st{sb % 3}")

            if g + 1 < NSB:
                emit_ln_stats(with_negm=False, skip_sq=True)

                def t_xhat_n(eng, nsl2=nsl2):
                    return eng.tensor_tensor(xhat[:], xr[:, nsl2], rb_bc,
                                             ALU.mult)
                sch.add("dve", t_xhat_n, reads=(r_x[nsl2], r_s5),
                        writes=(r_xhat,))

        pop_kv(len(kv_units))
        while v_defer:
            v_defer.pop(0)()
        pop_q(len(q_units))

        if g == 0:
            emit_qside_sq(0)
            emit_qside(0)
            emit_ksq(0)
            wins0 = [w for w in (-1, 0) if 0 <= w <= 30]
            for wi in range(len(wins0)):
                emit_negmh(0, 0, wi)
            emit_negmh(0, 1, 0)

            def t_cast1(eng):
                return eng.tensor_copy(xbv, xr[:, 1])
            sch.add("dve", t_cast1, reads=(r_x[1],), writes=(r_xbq,))
            emit_ln_stats(with_negm=False)

            def t_xhat1(eng):
                return eng.tensor_tensor(xhat[:], xr[:, 1], rb_bc, ALU.mult)
            sch.add("dve", t_xhat1, reads=(r_x[1], r_s5), writes=(r_xhat,))

    return nc, sch


def emit_program(nc, sch):
    counts = {e: 0 for e in Sched.ENGINES}
    dma_counts = {}
    for t in sch.tasks:
        if t.is_dma:
            dma_counts[t.chan] = dma_counts.get(t.chan, 0) + 16
            t.sem_val = dma_counts[t.chan]
        else:
            counts[t.eng] += 1
            t.sem_val = counts[t.eng]

    with contextlib.ExitStack() as st:
        sems = {e: st.enter_context(nc.semaphore(f"sem_{e}"))
                for e in Sched.ENGINES}
        dsems = {ch: st.enter_context(nc.semaphore(f"dma_{ch}"))
                 for ch in dma_counts}
        block = st.enter_context(nc.Block())

        def emit_engine(eng_name, eng):
            best = {}
            for t in sch.tasks:
                if t.eng != eng_name:
                    continue
                for d in sorted(t.deps):
                    dt = sch.tasks[d]
                    if dt.is_dma:
                        key = ("dma", dt.chan)
                        sem = dsems[dt.chan]
                    else:
                        key = dt.eng
                        sem = sems[dt.eng]
                    v = dt.sem_val
                    if best.get(key, 0) >= v:
                        continue
                    eng.wait_ge(sem, v)
                    best[key] = v
                last = t.emit(eng)
                assert last is not None, t.tid
                if t.is_dma:
                    last.then_inc(dsems[t.chan], 16)
                else:
                    last.then_inc(sems[t.eng], 1)
            if eng_name == "sync":
                for ch in ("st0", "st1", "st2"):
                    if ch in dma_counts:
                        eng.wait_ge(dsems[ch], dma_counts[ch])

        @block.tensor
        def _(eng):
            emit_engine("pe", eng)

        @block.scalar
        def _(eng):
            emit_engine("act", eng)

        @block.vector
        def _(eng):
            emit_engine("dve", eng)

        @block.gpsimd
        def _(eng):
            emit_engine("pool", eng)

        @block.sync
        def _(eng):
            emit_engine("sync", eng)
    return nc


_CACHED = {}


def _get_nc():
    if "nc" not in _CACHED:
        nc, sch = build_nc()
        emit_program(nc, sch)
        _CACHED["nc"] = nc
    return _CACHED["nc"]


def kernel(L0_attn_hidden_states, L0_attn_Wq, L0_attn_Wk, L0_attn_Wv,
           L0_attn_Wo, L0_ffn_U1, L0_ffn_V1, L0_ffn_b1, L0_ffn_U2,
           L0_ffn_V2, L0_ffn_b2):
    x = np.asarray(L0_attn_hidden_states, dtype=np.float32)
    weights = {
        "wq": np.ascontiguousarray(np.asarray(L0_attn_Wq, np.float32)),
        "wk": np.ascontiguousarray(np.asarray(L0_attn_Wk, np.float32)),
        "wv": np.ascontiguousarray(np.asarray(L0_attn_Wv, np.float32)),
        "wo": np.ascontiguousarray(np.asarray(L0_attn_Wo, np.float32)),
        "u1": np.ascontiguousarray(np.asarray(L0_ffn_U1, np.float32)),
        "v1": np.ascontiguousarray(np.asarray(L0_ffn_V1, np.float32)),
        "b1": np.ascontiguousarray(np.asarray(L0_ffn_b1, np.float32)),
        "u2": np.ascontiguousarray(np.asarray(L0_ffn_U2, np.float32)),
        "v2": np.ascontiguousarray(np.asarray(L0_ffn_V2, np.float32)),
        "b2": np.ascontiguousarray(np.asarray(L0_ffn_b2, np.float32)),
    }
    nc = _get_nc()
    in_maps = []
    for b in range(8):
        m = dict(weights)
        m["x"] = np.ascontiguousarray(x[b].T)
        in_maps.append(m)
    res = run_bass_kernel_spmd(nc, in_maps, core_ids=list(range(8)))
    out = np.empty((8, S, D), np.float32)
    for b in range(8):
        out[b] = res.results[b]["out"].T
    return out

